# version 02
# baseline (speedup 1.0000x reference)
import os
import numpy as np
import ml_dtypes
import concourse.bass as bass
import concourse.mybir as mybir
from concourse.bass import AxisInfo
from concourse.bass_utils import run_bass_kernel_spmd

F32 = mybir.dt.float32
BF16 = mybir.dt.bfloat16
I32 = mybir.dt.int32
AF = mybir.ActivationFunctionType
ALU = mybir.AluOpType
AX = mybir.AxisListType

D = 1024
T = 8192
B = 2
NCH = 8
EPS_LN = 1e-5
EPS_RMS = 1e-6

ENGINES = ("sync", "scalar", "tensor", "vector", "gpsimd")


class Prog:

    STRICT = os.environ.get("KSTRICT", "0") == "1"

    def __init__(self, nc):
        self.nc = nc
        self.ops = {e: [] for e in ENGINES}
        self.sems = {}
        self.cnt = {}
        self.waited = {e: {} for e in ENGINES}
        self._dma_i = {}
        self._ctx = []

    def sem(self, name):
        if name not in self.sems:
            cm = self.nc.semaphore(name)
            self.sems[name] = cm.__enter__()
            self._ctx.append(cm)
            self.cnt[name] = 0
        return self.sems[name]

    def sb(self, name, shape, dtype):
        cm = self.nc.sbuf_tensor(name, list(shape), dtype)
        h = cm.__enter__()
        self._ctx.append(cm)
        return h

    def psum(self, name, shape, dtype):
        cm = self.nc.psum_tensor(name, list(shape), dtype)
        h = cm.__enter__()
        self._ctx.append(cm)
        return h

    def op(self, eng, fn, after=(), inc=None, amt=1):
        if self.STRICT and fn is not None and eng in ("vector", "scalar", "gpsimd") \
                and inc is None:
            inc = esem(eng)
            amt = 1
        if self.STRICT and fn is not None and eng in ("vector", "scalar", "gpsimd"):
            o = esem(eng)
            self.sem(o)
            after = list(after) + [(o, self.cnt[o])]
        waits = []
        for tk in after:
            if tk is None:
                continue
            s, v = tk
            self.sem(s)
            if v > self.waited[eng].get(s, 0):
                self.waited[eng][s] = v
                waits.append((s, v))
        tok = None
        if inc is not None:
            self.sem(inc)
            self.cnt[inc] += amt
            tok = (inc, self.cnt[inc])
        self.ops[eng].append((fn, waits, inc, amt))
        return tok

    DMA_SEM_POOL = 8

    def dma(self, queue, out, in_, after=()):
        i = self._dma_i.setdefault(queue, 0)
        self._dma_i[queue] = i + 1
        K = self.DMA_SEM_POOL
        s = f"d{queue}{i % K}"
        rnd = i // K
        pre = [(s, 16 * rnd)] if rnd > 0 else []
        return self.op(queue, lambda h, o=out, i_=in_: h.dma_start(out=o, in_=i_),
                       after=list(after) + pre, inc=s, amt=16)

    def emit_and_close(self):
        with self.nc.Block() as block:
            for eng in ENGINES:
                if not self.ops[eng]:
                    continue

                def runner(h, _eng=eng):
                    for (fn, waits, inc, amt) in self.ops[_eng]:
                        for (s, v) in waits:
                            h.wait_ge(self.sems[s], v)
                        ins = fn(h) if fn is not None else None
                        if inc is not None:
                            ins.then_inc(self.sems[inc], amt)

                getattr(block, eng)(runner)
        for cm in reversed(self._ctx):
            cm.__exit__(None, None, None)
        self._ctx = []


def esem(eng):
    return {"vector": "dve", "gpsimd": "gp", "scalar": "act", "tensor": "pe"}[eng]



def emit_identity(p, ident, iota_scratch):
    t0 = p.op("gpsimd", lambda h: h.iota(iota_scratch[:], pattern=[[1, 128]], base=0,
                                         channel_multiplier=-1,
                                         allow_small_or_imprecise_dtypes=True), inc="gp")
    t1 = p.op("gpsimd", lambda h: h.tensor_single_scalar(ident[:], iota_scratch[:], 0.0,
                                                         ALU.is_equal), after=[t0], inc="gp")
    return t1


def tm_ln_emit(p, st, i, x_ap, out_ap, nfree, after=(), eng="vector",
               eps=EPS_LN, kind="ln", out_extra=None, gb=None, extra_after=()):
    b = 32 * i
    nh = (nfree + 511) // 512
    assert nh <= 4
    se = esem(eng)
    tstats = []
    for j in range(nh):
        lo = j * (nfree // nh)
        hi = (j + 1) * (nfree // nh)
        tstats.append(p.op(eng, lambda h, j=j, lo=lo, hi=hi: h.bn_stats(
            st[:, b + 6 * j:b + 6 * j + 6], x_ap[:, lo:hi]),
            after=after if j == 0 else (), inc=se))
    t3 = p.op(eng, lambda h: h.bn_aggr(st[:, b + 24:b + 26], st[:, b:b + 6 * nh]),
              after=tstats, inc=se)
    tprev = t3
    if kind == "rms":
        t4 = p.op(eng, lambda h: h.scalar_tensor_tensor(out=st[:, b + 26:b + 27],
                                                        in0=st[:, b + 24:b + 25],
                                                        scalar=1.0,
                                                        in1=st[:, b + 24:b + 25],
                                                        op0=ALU.mult, op1=ALU.mult),
                  after=[t3], inc=se)
        tprev = p.op(eng, lambda h: h.tensor_tensor(st[:, b + 25:b + 26],
                                                    st[:, b + 25:b + 26],
                                                    st[:, b + 26:b + 27], ALU.add),
                     after=[t4], inc=se)
    tv = p.op(eng, lambda h: h.tensor_scalar_add(st[:, b + 25:b + 26],
                                                 st[:, b + 25:b + 26], eps),
              after=[tprev], inc=se)
    ts = p.op("scalar", lambda h: h.activation(out=st[:, b + 26:b + 27],
                                               in_=st[:, b + 25:b + 26], func=AF.Sqrt),
              after=[tv], inc="act")
    trr = p.op(eng, lambda h: h.reciprocal(st[:, b + 27:b + 28], st[:, b + 26:b + 27]),
               after=[ts], inc=se)
    if kind == "ln":
        tok = p.op(eng, lambda h: h.tensor_scalar(out=out_ap, in0=x_ap,
                                                  scalar1=st[:, b + 24:b + 25],
                                                  scalar2=st[:, b + 27:b + 28],
                                                  op0=ALU.subtract, op1=ALU.mult),
                   after=[trr], inc=se)
    else:
        tok = p.op(eng, lambda h: h.tensor_scalar_mul(out_ap, x_ap, st[:, b + 27:b + 28]),
                   after=[trr], inc=se)
    if gb is not None:
        gba, bba, gtok = gb
        p.op(eng, lambda h: h.tensor_tensor(out_ap, out_ap, gba, ALU.mult),
             after=[gtok] if gtok else ())
        tok = p.op(eng, lambda h: h.tensor_tensor(out_ap, out_ap, bba, ALU.add), inc=se)
    if out_extra is not None:
        te = p.op("scalar", lambda h: h.copy(out=out_extra, in_=out_ap),
                  after=[tok] + list(extra_after), inc="act")
        return tok, te
    return tok


class Transposer:

    def __init__(self, p, ident_tok, ident, name="psT"):
        self.p = p
        self.ps = [p.psum(name + "a", [128, 1024], BF16),
                   p.psum(name + "b", [128, 1024], BF16)]
        self.ident = ident
        self.ident_tok = ident_tok
        self.hist = []

    def transpose(self, src_ap, dst_ap, after=(), evac_eng="vector", evac_fn=None):
        p = self.p
        k = len(self.hist)
        d = self.ps[k % 2][:, 0:128]
        aft = list(after) + [self.ident_tok]
        if k >= 2:
            aft.append(self.hist[k - 2])
        tmm = p.op("tensor", lambda h, s=src_ap, d=d: h.transpose(d, s, self.ident[:]),
                   after=aft, inc="pe")
        if evac_fn is None:
            fn = lambda h, d=d, o=dst_ap: h.tensor_copy(o, d)
        else:
            fn = lambda h, d=d, o=dst_ap: evac_fn(h, d, o)
        tev = p.op(evac_eng, fn, after=[tmm], inc=esem(evac_eng))
        self.hist.append(tev)
        return tev



def host_prep(inputs):
    g = {}
    t = lambda a: np.ascontiguousarray(np.asarray(a).T)
    c = lambda a: np.ascontiguousarray(np.asarray(a))
    g["W_inT"] = t(inputs["L0_attn_in_proj_w"])
    g["W_outT"] = t(inputs["L0_attn_out_proj_w"])
    g["l1T"] = t(inputs["L0_attn_l1_w"])
    g["l2T"] = t(inputs["L0_attn_l2_w"])
    g["l3T"] = t(inputs["L0_attn_l3_w"])
    for k in ("q_w", "k_w", "v_w", "out_w", "gate_w", "moe_gate_w", "w1", "w2"):
        g[k] = c(inputs["L0_ffn_" + k])
    g["scale_w"] = c(inputs["L0_ffn_scale_w"])
    g["b1"] = c(inputs["L0_ffn_b1"])
    g["ln_g"] = c(inputs["L0_ffn_ln_g"])
    g["ln_b"] = c(inputs["L0_ffn_ln_b"])
    g["moe_gate_b"] = c(inputs["L0_ffn_moe_gate_b"]).reshape(8, 1)
    g["res_scale"] = c(inputs["L0_ffn_res_scale"]).reshape(8, 1)
    g["b2"] = c(inputs["L0_ffn_b2"])
    g["WkT"] = t(inputs["L1_attn_Wk"])
    g["WvT"] = t(inputs["L1_attn_Wv"])
    g["WrT"] = t(inputs["L1_attn_Wr"])
    g["WoT"] = t(inputs["L1_attn_Wo"])
    g["WfkT"] = t(inputs["L1_attn_Wfk"])
    g["WfrT"] = t(inputs["L1_attn_Wfr"])
    g["WfvT"] = t(inputs["L1_attn_Wfv"])
    g["f_gateT"] = t(inputs["L1_ffn_gate_w"])
    g["f_w1"] = c(inputs["L1_ffn_w1"])
    g["f_w2"] = c(inputs["L1_ffn_w2"])
    g["f_w3"] = c(inputs["L1_ffn_w3"])
    g["f_w1s"] = c(inputs["L1_ffn_w1s"][0])
    g["f_w2s"] = c(inputs["L1_ffn_w2s"][0])
    g["f_w3s"] = c(inputs["L1_ffn_w3s"][0])

    cols = {}

    def add(name, arr):
        arr = np.asarray(arr)
        assert arr.shape == (D,), (name, arr.shape)
        cols[name] = arr.astype(np.float32)

    add("pre_w", inputs["L0_attn_pre_norm_w"])
    add("post_w", inputs["L0_attn_post_norm_w"])
    add("D_skip", inputs["L0_attn_D_skip"])
    sbv = np.asarray(inputs["L0_attn_short_b"])
    add("short_b_x2", sbv[0:D]); add("short_b_x1", sbv[D:2 * D]); add("short_b_v", sbv[2 * D:3 * D])
    sw = np.asarray(inputs["L0_attn_short_w"]).reshape(3 * D, 3)
    for gname, base in (("x2", 0), ("x1", D), ("v", 2 * D)):
        for j in range(3):
            add(f"short_w_{gname}_{j}", sw[base:base + D, j])
    lp = np.asarray(inputs["L0_attn_log_poles"]).reshape(D, 16)
    rs = np.asarray(inputs["L0_attn_residues"])
    for s in range(16):
        add(f"log_pole_{s}", lp[:, s])
        add(f"residue_{s}", rs[:, s])
    add("gate_b", inputs["L0_ffn_gate_b"])
    add("n1_g", inputs["L0_ffn_n1_g"]); add("n1_b", inputs["L0_ffn_n1_b"])
    add("n2_g", inputs["L0_ffn_n2_g"]); add("n2_b", inputs["L0_ffn_n2_b"])
    add("ln0_w", inputs["L1_attn_ln0_w"]); add("ln0_b", inputs["L1_attn_ln0_b"])
    add("ln1_w", inputs["L1_attn_ln1_w"]); add("ln1_b", inputs["L1_attn_ln1_b"])
    add("ln2_w", inputs["L1_attn_ln2_w"]); add("ln2_b", inputs["L1_attn_ln2_b"])
    add("time_decay", inputs["L1_attn_time_decay"])
    add("time_first", inputs["L1_attn_time_first"])
    add("tmk", inputs["L1_attn_tmk"]); add("tmv", inputs["L1_attn_tmv"]); add("tmr", inputs["L1_attn_tmr"])
    add("cmk", inputs["L1_attn_cmk"]); add("cmr", inputs["L1_attn_cmr"])
    names = sorted(cols)
    g["vec_names"] = names
    g["NV"] = len(names)
    g["vecs"] = np.ascontiguousarray(np.stack([cols[n] for n in names], axis=1))
    u = np.asarray(inputs["L0_attn_u"])
    up = np.zeros((B, 128 + T, D), dtype=np.float32)
    up[:, 128:, :] = u
    g["u_pad"] = up
    return g



def build_p0(Nt, NV, vc, stage=None):
    nc = bass.Bass("TRN2", target_bir_lowering=False,
                   detect_race_conditions=Prog.STRICT)
    NtE = Nt + 128
    ntile = NtE // 128
    u_in = nc.dram_tensor("u_blk", [NtE, D], F32, kind="ExternalInput").ap()
    w_in = nc.dram_tensor("W_inT", [D, 3072], F32, kind="ExternalInput").ap()
    vecs_in = nc.dram_tensor("vecs", [D, NV], F32, kind="ExternalInput").ap()
    xv_out = nc.dram_tensor("xv_fm", [D, Nt], F32, kind="ExternalOutput").ap()
    x2_out = nc.dram_tensor("x2_fm", [D, Nt], F32, kind="ExternalOutput").ap()
    xnfm_out = nc.dram_tensor("xn0_fm_b", [D, Nt], BF16, kind="ExternalOutput").ap()
    xntm_out = nc.dram_tensor("xn0_tm", [Nt, D], F32, kind="ExternalOutput").ap()

    p = Prog(nc)
    ZW = 260
    wT = p.sb("wT", [128, NCH * 3072], BF16)
    vecs = p.sb("vecs_sb", [128, NCH * NV], F32)
    xt = p.sb("xt", [128, 2 * D], F32)
    xnt = p.sb("xnt", [128, 2 * D], F32)
    rnt = p.sb("rnt", [128, 2 * D], F32)
    xnbt = p.sb("xnbt", [128, 2 * D], BF16)
    xnbt2 = p.sb("xnbt2", [128, 2 * D], BF16)
    stats = p.sb("stats", [128, 32 * 2 * ntile], F32)
    rnfm = p.sb("rnfm", [128, NCH * NtE], BF16)
    xstg = p.sb("xstg", [128, 2 * 128], BF16)
    zbuf = p.sb("zbuf", [128, 24 * ZW], F32)
    cx1 = p.sb("cx1", [128, NCH * 256], F32)
    cv = p.sb("cv", [128, NCH * 256], F32)
    x2b = p.sb("x2b", [128, NCH * 256], F32)
    cvt = p.sb("cvt", [128, NCH * 256], F32)
    ident = p.sb("ident", [128, 128], BF16)
    iota2 = p.sb("iota2", [128, 128], F32)

    psA = p.psum("psA", [128, 512], F32)
    psB = p.psum("psB", [128, 512], F32)

    tok_w = [p.dma("gpsimd", wT[:, c * 3072:(c + 1) * 3072], w_in[c * 128:(c + 1) * 128, :])
             for c in range(NCH)]
    tok_v = [p.dma("gpsimd", vecs[:, c * NV:(c + 1) * NV], vecs_in[c * 128:(c + 1) * 128, :])
             for c in range(NCH)]
    tid = emit_identity(p, ident, iota2)
    tr = Transposer(p, tid, ident)

    stage = stage or os.environ.get("P0STAGE", "full")
    tok_ws = []
    for c in range(NCH):
        pw = vecs[:, c * NV + vc["pre_w"]: c * NV + vc["pre_w"] + 1]
        tok_ws.append(p.op("vector", lambda h, c=c, pw=pw: h.tensor_scalar_mul(
            wT[:, c * 3072:(c + 1) * 3072], wT[:, c * 3072:(c + 1) * 3072], pw),
            after=[tok_w[c], tok_v[c]], inc="dve"))

    fm_tok = {}
    fm2_tok = {}
    guards = {}
    all_xntm_stores = []
    for i in range(ntile):
        sl = i % 2
        xti = xt[:, sl * D:(sl + 1) * D]
        xni = xnt[:, sl * D:(sl + 1) * D]
        rni = rnt[:, sl * D:(sl + 1) * D]
        xbi = xnbt[:, sl * D:(sl + 1) * D]
        xbi2 = xnbt2[:, sl * D:(sl + 1) * D]
        g = guards.get(sl, {})
        tu = p.dma("sync", xti, u_in[i * 128:(i + 1) * 128, :], after=[g.get("xt")])
        tln = tm_ln_emit(p, stats, 2 * i, xti, xni, D, after=[tu, g.get("xn")], kind="ln")
        tb2 = p.op("scalar", lambda h, xni=xni, xbi2=xbi2: h.copy(out=xbi2, in_=xni),
                   after=[tln, g.get("xb2")], inc="act")
        trn, tb = tm_ln_emit(p, stats, 2 * i + 1, xni, rni, D,
                             after=[tln, g.get("rn")], kind="rms", eps=EPS_RMS,
                             out_extra=xbi, extra_after=[g.get("xb")])
        tst = None
        if i >= 1:
            tst = p.dma("scalar", xntm_out[(i - 1) * 128:i * 128, :], xni, after=[tln])
            all_xntm_stores.append(tst)
        trs1 = []
        trs2 = []
        for c in (range(NCH) if stage != "ln" else []):
            dst = rnfm[:, c * NtE + i * 128: c * NtE + (i + 1) * 128]
            tk = tr.transpose(xbi[:, c * 128:(c + 1) * 128], dst, after=[tb])
            fm_tok[(i, c)] = tk
            trs1.append(tk)
            if i >= 1:
                ssl = (i * NCH + c) % 2
                dst2 = xstg[:, ssl * 128:(ssl + 1) * 128]
                tk2 = tr.transpose(xbi2[:, c * 128:(c + 1) * 128], dst2,
                                   after=[tb2, fm2_tok.get((i, c, "st"))])
                if stage == "trnostore":
                    fm2_tok[((i * NCH + c + 2) // NCH, (i * NCH + c + 2) % NCH, "st")] = tk2
                    trs2.append(tk2)
                    continue
                tst2 = p.dma("scalar", xnfm_out[c * 128:(c + 1) * 128,
                                                (i - 1) * 128:i * 128], dst2, after=[tk2])
                step = i * NCH + c
                nstep = step + 2
                fm2_tok[(nstep // NCH, nstep % NCH, "st")] = tst2
                all_xntm_stores.append(tst2)
                tk2 = tst2
            else:
                tk2 = None
            trs2.append(tk2)
        if stage == "ln":
            guards[sl] = {"xt": tln, "xn": tst or trn, "rn": tb, "xb": None,
                          "xb2": tb2}
        else:
            guards[sl] = {"xt": tln, "xn": tst or trn, "rn": tb, "xb": trs1[-1],
                          "xb2": trs2[-1] or tb2}

    fm_store = []
    blks = [(0, 128)] + [(128 + j * 256, 256) for j in range(Nt // 256)]
    if stage in ("ln", "tr", "trnostore"):
        blks = []
    psum_free = {}
    conv_done = {}
    tail_done = {}
    store_prev = {}
    all_stores = fm_store[:] + all_xntm_stores
    for bi, (bs, bl) in enumerate(blks):
        tiles_in_blk = range(bs // 128, (bs + bl) // 128)
        rhs_dep = [fm_tok[(i, c)] for i in tiles_in_blk for c in range(NCH)]
        for f in range(24):
            ps, psn = (psA, "A") if (bi * 24 + f) % 2 == 0 else (psB, "B")
            for c in range(NCH):
                lhs = wT[:, c * 3072 + f * 128: c * 3072 + (f + 1) * 128]
                rhs = rnfm[:, c * NtE + bs: c * NtE + bs + bl]
                aft = []
                if c == 0:
                    aft = rhs_dep + tok_ws + ([psum_free[psn]] if psn in psum_free else [])
                p.op("tensor", lambda h, l=lhs, r=rhs, ps=ps, c=c, bl=bl: h.matmul(
                    ps[:, 0:bl], l, r, start=(c == 0), stop=(c == NCH - 1)),
                    after=aft, inc="pe" if c == NCH - 1 else None)
            tmm = ("pe", p.cnt["pe"])
            zdst = zbuf[:, f * ZW + 2: f * ZW + 2 + bl]
            aft_ev = [tmm, conv_done.get((f, bi - 1)), tail_done.get((f, bi - 1))]
            tev = p.op("scalar", lambda h, ps=ps, zdst=zdst, bl=bl: h.copy(
                out=zdst, in_=ps[:, 0:bl]), after=aft_ev, inc="act")
            psum_free[psn] = tev
            eng = "vector" if f % 2 == 0 else "gpsimd"
            se = esem(eng)
            if bi > 0 and stage == "full":
                grp, c = f // 8, f % 8
                gname = ("x2", "x1", "v")[grp]
                w0 = vecs[:, c * NV + vc[f"short_w_{gname}_0"]:][:, 0:1]
                w1 = vecs[:, c * NV + vc[f"short_w_{gname}_1"]:][:, 0:1]
                w2 = vecs[:, c * NV + vc[f"short_w_{gname}_2"]:][:, 0:1]
                bvec = vecs[:, c * NV + vc[f"short_b_{gname}"]:][:, 0:1]
                dstbuf = (x2b, cx1, cv)[grp]
                dst = dstbuf[:, c * 256: c * 256 + bl]
                z0 = zbuf[:, f * ZW + 0: f * ZW + bl]
                z1 = zbuf[:, f * ZW + 1: f * ZW + 1 + bl]
                z2 = zbuf[:, f * ZW + 2: f * ZW + 2 + bl]
                if eng == "vector":
                    p.op(eng, lambda h, dst=dst, z0=z0, w0=w0, bvec=bvec: h.tensor_scalar(
                        out=dst, in0=z0, scalar1=w0, scalar2=bvec, op0=ALU.mult, op1=ALU.add),
                        after=[tev, store_prev.get((grp, c))])
                    p.op(eng, lambda h, dst=dst, z1=z1, w1=w1: h.scalar_tensor_tensor(
                        out=dst, in0=z1, scalar=w1, in1=dst, op0=ALU.mult, op1=ALU.add))
                    tcv = p.op(eng, lambda h, dst=dst, z2=z2, w2=w2: h.scalar_tensor_tensor(
                        out=dst, in0=z2, scalar=w2, in1=dst, op0=ALU.mult, op1=ALU.add),
                        inc=se)
                else:
                    w0b = w0.broadcast_to([128, bl]); w1b = w1.broadcast_to([128, bl])
                    w2b = w2.broadcast_to([128, bl]); bvb = bvec.broadcast_to([128, bl])
                    p.op(eng, lambda h, dst=dst, z0=z0, w0b=w0b: h.tensor_tensor(
                        dst, z0, w0b, ALU.mult), after=[tev, store_prev.get((grp, c))])
                    p.op(eng, lambda h, dst=dst, bvb=bvb: h.tensor_tensor(
                        dst, dst, bvb, ALU.add))
                    tmp = cvt[:, c * 256: c * 256 + bl]
                    p.op(eng, lambda h, tmp=tmp, z1=z1, w1b=w1b: h.tensor_tensor(
                        tmp, z1, w1b, ALU.mult))
                    p.op(eng, lambda h, dst=dst, tmp=tmp: h.tensor_tensor(
                        dst, dst, tmp, ALU.add))
                    p.op(eng, lambda h, tmp=tmp, z2=z2, w2b=w2b: h.tensor_tensor(
                        tmp, z2, w2b, ALU.mult))
                    tcv = p.op(eng, lambda h, dst=dst, tmp=tmp: h.tensor_tensor(
                        dst, dst, tmp, ALU.add), inc=se)
                conv_done[(f, bi)] = tcv
            tt = p.op(eng, lambda h, f=f, bl=bl: h.tensor_copy(
                zbuf[:, f * ZW + 0: f * ZW + 2],
                zbuf[:, f * ZW + bl: f * ZW + bl + 2]),
                after=[tev, conv_done.get((f, bi))], inc=se)
            tail_done[(f, bi)] = tt
        if bi == 0 or stage != "full":
            continue
        out0 = bs - 128
        for c in range(NCH):
            tg = p.op("vector", lambda h, c=c, bl=bl: h.tensor_tensor(
                cx1[:, c * 256: c * 256 + bl], cx1[:, c * 256: c * 256 + bl],
                cv[:, c * 256: c * 256 + bl], ALU.mult),
                after=[conv_done[(8 + c, bi)], conv_done[(16 + c, bi)]], inc="dve")
            s1 = p.dma("sync", xv_out[c * 128:(c + 1) * 128, out0:out0 + bl],
                       cx1[:, c * 256: c * 256 + bl], after=[tg])
            s2 = p.dma("sync", x2_out[c * 128:(c + 1) * 128, out0:out0 + bl],
                       x2b[:, c * 256: c * 256 + bl], after=[conv_done[(c, bi)]])
            all_stores += [s1, s2]
            store_prev[(1, c)] = s1
            store_prev[(2, c)] = tg
            store_prev[(0, c)] = s2
    p.op("gpsimd", None, after=all_stores)
    p.emit_and_close()
    return nc



def build_p1(CS, NV, vc):
    nc = bass.Bass("TRN2", target_bir_lowering=False,
                   detect_race_conditions=Prog.STRICT)
    xv_in = nc.dram_tensor("xv_fm", [CS, T], F32, kind="ExternalInput").ap()
    x2_in = nc.dram_tensor("x2_fm", [CS, T], F32, kind="ExternalInput").ap()
    vecs_in = nc.dram_tensor("vecs", [CS, NV], F32, kind="ExternalInput").ap()
    y_out = nc.dram_tensor("y0g_fm", [CS, T], F32, kind="ExternalOutput").ap()

    p = Prog(nc)
    nsub = CS // 128
    xv = p.sb("xv", [128, T], F32)
    ub = p.sb("ub", [128, 2 * T], F32)
    tmp = p.sb("tmp", [128, T], F32)
    yb = p.sb("yb", [128, T], F32)
    vecs = p.sb("vecs_sb", [128, NV], F32)
    A = p.sb("A", [128, 16], F32)
    x2t = p.sb("x2t", [128, 2 * 2048], F32)
    t2s = p.sb("t2s", [128, 2048], F32)

    all_stores = []
    prev = {}
    for sub in range(nsub):
        tv = p.dma("gpsimd", vecs[:], vecs_in[sub * 128:(sub + 1) * 128, :],
                   after=[prev.get("vecs")])
        txv = p.dma("sync", xv[:], xv_in[sub * 128:(sub + 1) * 128, :],
                    after=[prev.get("xv")])
        ta = []
        for s in range(16):
            c = vc[f"log_pole_{s}"]
            ta.append(p.op("scalar", lambda h, s=s, c=c: h.activation(
                out=A[:, s:s + 1], in_=vecs[:, c:c + 1], func=AF.Exp),
                after=[tv, prev.get("A")], inc="act"))
        tacc = None
        tscale_prev = None
        for s in range(16):
            u_ap = ub[:, (s % 2) * T:(s % 2) * T + T]
            tsc = p.op("vector", lambda h, u_ap=u_ap, s=s: h.tensor_tensor_scan(
                u_ap, A[:, s:s + 1].broadcast_to([128, T]), xv[:], 0.0,
                ALU.mult, ALU.add),
                after=[txv, ta[s], prev.get(("scale", s - 2))], inc="dve")
            rc = vc[f"residue_{s}"]
            tscl = p.op("scalar", lambda h, u_ap=u_ap, rc=rc: h.activation(
                out=tmp[:], in_=u_ap, func=AF.Copy, scale=vecs[:, rc:rc + 1]),
                after=[tsc, tacc], inc="act")
            prev[("scale", s)] = tscl
            if s == 0:
                tacc = p.op("gpsimd", lambda h: h.tensor_copy(yb[:], tmp[:]),
                            after=[tscl] + list(prev.get("yb4", ())), inc="gp")
            else:
                tacc = p.op("gpsimd", lambda h: h.tensor_tensor(
                    yb[:], yb[:], tmp[:], ALU.add), after=[tscl], inc="gp")
        dc = vc["D_skip"]
        for ch in range(4):
            sl = ch % 2
            x2a = x2t[:, sl * 2048:(sl + 1) * 2048]
            tx2 = p.dma("scalar", x2a, x2_in[sub * 128:(sub + 1) * 128,
                                             ch * 2048:(ch + 1) * 2048],
                        after=[prev.get(("x2", ch - 2)), tv])
            cl = slice(ch * 2048, (ch + 1) * 2048)
            tsk = p.op("scalar", lambda h, cl=cl: h.activation(
                out=t2s[:], in_=xv[:, cl], func=AF.Copy, scale=vecs[:, dc:dc + 1]),
                after=[tacc, prev.get(("t2s", ch - 1))], inc="act")
            t5 = p.op("vector", lambda h, cl=cl: h.tensor_tensor(
                yb[:, cl], yb[:, cl], t2s[:], ALU.add), after=[tsk], inc="dve")
            prev[("t2s", ch)] = t5
            t6 = p.op("vector", lambda h, cl=cl, x2a=x2a: h.tensor_tensor(
                yb[:, cl], yb[:, cl], x2a, ALU.mult), after=[tx2], inc="dve")
            prev[("x2", ch)] = t6
            tst = p.dma("sync", y_out[sub * 128:(sub + 1) * 128, cl], yb[:, cl],
                        after=[t6])
            all_stores.append(tst)
            prev[("ystore", ch)] = tst
        prev["vecs"] = prev[("t2s", 3)]
        prev["A"] = prev[("scale", 15)]
        prev["xv"] = prev[("t2s", 3)]
        prev["yb4"] = tuple(all_stores[-4:])
        prev[("scale", -2)] = prev[("scale", 14)]
        prev[("scale", -1)] = prev[("scale", 15)]
        prev[("x2", -2)] = prev[("x2", 2)]
        prev[("x2", -1)] = prev[("x2", 3)]
        prev[("t2s", -1)] = prev[("t2s", 3)]
    p.op("gpsimd", None, after=all_stores)
    p.emit_and_close()
    return nc



def build_p2(Nt, NV, vc):
    nc = bass.Bass("TRN2", target_bir_lowering=False,
                   detect_race_conditions=Prog.STRICT)
    y0g_in = nc.dram_tensor("y0g_fm", [D, Nt], F32, kind="ExternalInput").ap()
    u_in = nc.dram_tensor("u_rows", [Nt, D], F32, kind="ExternalInput").ap()
    wo_in = nc.dram_tensor("W_outT", [D, D], F32, kind="ExternalInput").ap()
    l1_in = nc.dram_tensor("l1T", [D, 2688], F32, kind="ExternalInput").ap()
    l2_in = nc.dram_tensor("l2T", [D, 2688], F32, kind="ExternalInput").ap()
    l3_in = nc.dram_tensor("l3T", [2688, D], F32, kind="ExternalInput").ap()
    vecs_in = nc.dram_tensor("vecs", [D, NV], F32, kind="ExternalInput").ap()
    s0_out = nc.dram_tensor("s0_tm", [Nt, D], F32, kind="ExternalOutput").ap()
    yb_scr = nc.dram_tensor("yb_scr", [Nt, D], F32, kind="Internal").ap()

    p = Prog(nc)
    ntile = Nt // 128
    NF = 21
    y0gb = p.sb("y0gb", [128, NCH * Nt], BF16)
    wob = p.sb("wob", [128, NCH * D], BF16)
    vecs = p.sb("vecs_sb", [128, NCH * NV], F32)
    ybt = p.sb("ybt", [128, 2 * D], F32)
    zmt = p.sb("zmt", [128, 2 * D], F32)
    zmb = p.sb("zmb", [128, 2 * D], BF16)
    zmfm = p.sb("zmfm", [128, NCH * Nt], BF16)
    stats = p.sb("stats", [128, 32 * ntile], F32)
    ut = p.sb("ut", [128, 2 * D], F32)
    l1b = p.sb("l1b", [128, 2 * NCH * 128], BF16)
    l2b = p.sb("l2b", [128, 2 * NCH * 128], BF16)
    l3b = p.sb("l3b", [128, NF * D], BF16)
    m1 = p.sb("m1", [128, 512], BF16)
    m2 = p.sb("m2", [128, 512], BF16)
    mbuf = p.sb("mbuf", [128, NF * 512], BF16)
    ident = p.sb("ident", [128, 128], BF16)
    iota2 = p.sb("iota2", [128, 128], F32)
    ytl = p.sb("ytl", [128, 2 * D], F32)

    psA = p.psum("psA", [128, 512], F32)
    psB = p.psum("psB", [128, 512], F32)
    ps0 = [p.psum(f"ps0_{i}", [128, 512], F32) for i in range(4)]

    tid = emit_identity(p, ident, iota2)
    tr = Transposer(p, tid, ident)

    tok_v = [p.dma("gpsimd", vecs[:, c * NV:(c + 1) * NV], vecs_in[c * 128:(c + 1) * 128, :])
             for c in range(NCH)]
    tok_wo = [p.dma("gpsimd", wob[:, c * D:(c + 1) * D], wo_in[c * 128:(c + 1) * 128, :])
              for c in range(NCH)]
    tok_l3 = [p.dma("gpsimd", l3b[:, f * D:(f + 1) * D], l3_in[f * 128:(f + 1) * 128, :])
              for f in range(NF)]

    def stripe_ap(w_in, f):
        return w_in.rearrange("(c p) q -> p (c q)", c=NCH)[:, :].rearrange(
            "p (c q) -> p c q", c=NCH)[:, :, f * 128:(f + 1) * 128]

    tok_ycast = [p.dma("gpsimd", y0gb[:, c * Nt:(c + 1) * Nt],
                       y0g_in[c * 128:(c + 1) * 128, :]) for c in range(NCH)]

    fm_tok = {}
    guards = {}
    for i in range(ntile):
        sl = i % 2
        ybi = ybt[:, sl * D:(sl + 1) * D]
        zmi = zmt[:, sl * D:(sl + 1) * D]
        zbi = zmb[:, sl * D:(sl + 1) * D]
        uti = ut[:, sl * D:(sl + 1) * D]
        g = guards.get(sl, {})
        tu = p.dma("sync", uti, u_in[i * 128:(i + 1) * 128, :], after=[g.get("ut")])
        for half in range(2):
            ps = psA if half == 0 else psB
            for c in range(NCH):
                lhs = y0gb[:, c * Nt + i * 128: c * Nt + (i + 1) * 128]
                rhs = wob[:, c * D + half * 512: c * D + half * 512 + 512]
                aft = []
                if c == 0:
                    aft = tok_ycast + tok_wo + [g.get(f"ps{half}")]
                p.op("tensor", lambda h, l=lhs, r=rhs, ps=ps, c=c: h.matmul(
                    ps[:], l, r, start=(c == 0), stop=(c == NCH - 1)),
                    after=aft, inc="pe" if c == NCH - 1 else None)
            tmm = ("pe", p.cnt["pe"])
            tev = p.op("vector", lambda h, ps=ps, ybi=ybi, half=half, uti=uti: h.tensor_tensor(
                ybi[:, half * 512:(half + 1) * 512], ps[:],
                uti[:, half * 512:(half + 1) * 512], ALU.add),
                after=[tmm, tu, g.get("yb")], inc="dve")
            guards.setdefault(sl, {})[f"ps{half}"] = tev
        tyb = tev
        tsc = p.dma("scalar", yb_scr[i * 128:(i + 1) * 128, :], ybi, after=[tyb])
        trm = tm_ln_emit(p, stats, i, ybi, zmi, D, after=[tyb, g.get("zm")],
                         kind="rms", eps=EPS_RMS)
        tcb = p.op("scalar", lambda h, zmi=zmi, zbi=zbi: h.copy(out=zbi, in_=zmi),
                   after=[trm, g.get("zmb")], inc="act")
        trs = []
        for c in range(NCH):
            dst = zmfm[:, c * Nt + i * 128: c * Nt + (i + 1) * 128]
            pw = vecs[:, c * NV + vc["post_w"]: c * NV + vc["post_w"] + 1]
            trs.append(tr.transpose(
                zbi[:, c * 128:(c + 1) * 128], dst,
                after=[tcb, tok_v[c]],
                evac_fn=lambda h, d, o, pw=pw: h.tensor_scalar_mul(o, d, pw)))
        guards[sl] = {"ut": tev, "yb": tsc, "zm": tcb, "zmb": trs[-1],
                      "ps0": guards[sl].get("ps0"), "ps1": guards[sl].get("ps1")}
        fm_tok[i] = trs
    all_fm = [t for i in fm_tok for t in fm_tok[i]]

    all_stores = []
    prevg = {}
    for grp in range(Nt // 512):
        cols = slice(grp * 512, (grp + 1) * 512)
        mtoks = []
        for f in range(NF):
            sl = f % 2
            l1s = l1b[:, sl * NCH * 128:(sl + 1) * NCH * 128]
            l2s = l2b[:, sl * NCH * 128:(sl + 1) * NCH * 128]
            t1 = p.dma("gpsimd", l1s,
                       l1_in.rearrange("(c p) q -> p c q", c=NCH)[:, :, f * 128:(f + 1) * 128],
                       after=[prevg.get(("l1w", sl))])
            t2 = p.dma("gpsimd", l2s,
                       l2_in.rearrange("(c p) q -> p c q", c=NCH)[:, :, f * 128:(f + 1) * 128],
                       after=[prevg.get(("l2w", sl))])
            for c in range(NCH):
                lhs = l1s[:, c * 128:(c + 1) * 128]
                aft = []
                if c == 0:
                    aft = all_fm + [t1, prevg.get("psA")]
                p.op("tensor", lambda h, l=lhs, ps=psA, c=c, cols=cols: h.matmul(
                    ps[:], l, zmfm[:, c * Nt + cols.start: c * Nt + cols.stop],
                    start=(c == 0), stop=(c == NCH - 1)),
                    after=aft, inc="pe" if c == NCH - 1 else None)
            tmm1 = ("pe", p.cnt["pe"])
            prevg[("l1w", sl)] = tmm1
            tg1 = p.op("scalar", lambda h: h.activation(out=m1[:], in_=psA[:],
                                                        func=AF.Gelu_apprx_tanh),
                       after=[tmm1, prevg.get("m1")], inc="act")
            prevg["psA"] = tg1
            for c in range(NCH):
                lhs = l2s[:, c * 128:(c + 1) * 128]
                aft = []
                if c == 0:
                    aft = [t2, prevg.get("psB")]
                p.op("tensor", lambda h, l=lhs, ps=psB, c=c, cols=cols: h.matmul(
                    ps[:], l, zmfm[:, c * Nt + cols.start: c * Nt + cols.stop],
                    start=(c == 0), stop=(c == NCH - 1)),
                    after=aft, inc="pe" if c == NCH - 1 else None)
            tmm2 = ("pe", p.cnt["pe"])
            prevg[("l2w", sl)] = tmm2
            tg2 = p.op("scalar", lambda h: h.copy(out=m2[:], in_=psB[:]),
                       after=[tmm2, prevg.get("m2")], inc="act")
            prevg["psB"] = tg2
            tm_ = p.op("vector", lambda h, f=f: h.tensor_tensor(
                mbuf[:, f * 512:(f + 1) * 512], m1[:], m2[:], ALU.mult),
                after=[tg1, tg2, prevg.get(("mb", f))], inc="dve")
            prevg["m1"] = tm_
            prevg["m2"] = tm_
            mtoks.append(tm_)
        for ii in range(4):
            i = grp * 4 + ii
            for half in range(2):
                ps = ps0[(ii % 2) * 2 + half]
                for f in range(NF):
                    lhs = mbuf[:, f * 512 + ii * 128: f * 512 + (ii + 1) * 128]
                    rhs = l3b[:, f * D + half * 512: f * D + half * 512 + 512]
                    aft = []
                    if f == 0:
                        aft = mtoks + tok_l3 + [prevg.get(("ps0", (ii % 2) * 2 + half))]
                    p.op("tensor", lambda h, l=lhs, r=rhs, ps=ps, f=f: h.matmul(
                        ps[:], l, r, start=(f == 0), stop=(f == NF - 1)),
                        after=aft, inc="pe" if f == NF - 1 else None)
                tmm = ("pe", p.cnt["pe"])
                if half == 0:
                    sl = i % 2
                    tyl = p.dma("scalar", ytl[:, sl * D:(sl + 1) * D],
                                yb_scr[i * 128:(i + 1) * 128, :],
                                after=[prevg.get(("ytl", sl))])
                tvv = p.op("vector", lambda h, ps=ps, i=i, half=half, sl=i % 2: h.tensor_tensor(
                    ytl[:, sl * D + half * 512: sl * D + half * 512 + 512],
                    ytl[:, sl * D + half * 512: sl * D + half * 512 + 512],
                    ps[:], ALU.add), after=[tmm, tyl], inc="dve")
                prevg[("ps0", (ii % 2) * 2 + half)] = tvv
                if half == 1:
                    tst = p.dma("sync", s0_out[i * 128:(i + 1) * 128, :],
                                ytl[:, (i % 2) * D:(i % 2) * D + D],
                                after=[prevg[("ps0", (ii % 2) * 2)], tvv])
                    all_stores.append(tst)
                    prevg[("ytl", i % 2)] = tst
        for f in range(NF):
            prevg[("mb", f)] = ("pe", p.cnt["pe"])
    p.op("gpsimd", None, after=all_stores)
    p.emit_and_close()
    return nc



def build_p3(Nt):
    nc = bass.Bass("TRN2", target_bir_lowering=False,
                   detect_race_conditions=Prog.STRICT)
    xn_in = nc.dram_tensor("xn0_fm_b", [D, Nt], BF16, kind="ExternalInput").ap()
    qw_in = nc.dram_tensor("q_w", [D, D], F32, kind="ExternalInput").ap()
    kw_in = nc.dram_tensor("k_w", [D, D], F32, kind="ExternalInput").ap()
    vw_in = nc.dram_tensor("v_w", [D, D], F32, kind="ExternalInput").ap()
    q_out = nc.dram_tensor("q_fm_b", [D, Nt], BF16, kind="ExternalOutput").ap()
    k_out = nc.dram_tensor("k_fm_b", [D, Nt], BF16, kind="ExternalOutput").ap()
    v_out = nc.dram_tensor("v_tm_b", [Nt, D], BF16, kind="ExternalOutput").ap()

    p = Prog(nc)
    ntile = Nt // 128
    xnfm = p.sb("xnfm", [128, NCH * Nt], BF16)
    qw = p.sb("qw", [128, NCH * D], BF16)
    kw = p.sb("kw", [128, NCH * D], BF16)
    vw = p.sb("vw", [128, NCH * D], BF16)
    stg = p.sb("stg", [128, 4 * 512], BF16)

    psA = p.psum("psA", [128, 512], F32)
    psB = p.psum("psB", [128, 512], F32)

    t_xn = [p.dma("sync", xnfm[:, c * Nt:(c + 1) * Nt], xn_in[c * 128:(c + 1) * 128, :])
            for c in range(NCH)]
    t_qw = [p.dma("gpsimd", qw[:, c * D:(c + 1) * D], qw_in[c * 128:(c + 1) * 128, :])
            for c in range(NCH)]
    t_kw = [p.dma("gpsimd", kw[:, c * D:(c + 1) * D], kw_in[c * 128:(c + 1) * 128, :])
            for c in range(NCH)]
    t_vw = [p.dma("gpsimd", vw[:, c * D:(c + 1) * D], vw_in[c * 128:(c + 1) * 128, :])
            for c in range(NCH)]
    stores = []
    prevp = {}
    sidx = 0

    def mm_group(lhs_fn, rhs_fn, out_ap, wtoks, tag):
        nonlocal sidx
        ps = psA if tag % 2 == 0 else psB
        for c in range(NCH):
            aft = []
            if c == 0:
                aft = t_xn + wtoks + [prevp.get(tag % 2)]
            p.op("tensor", lambda h, l=lhs_fn(c), r=rhs_fn(c), ps=ps, c=c: h.matmul(
                ps[:], l, r, start=(c == 0), stop=(c == NCH - 1)),
                after=aft, inc="pe" if c == NCH - 1 else None)
        tmm = ("pe", p.cnt["pe"])
        sl = sidx % 4
        sidx += 1
        sa = stg[:, sl * 512:(sl + 1) * 512]
        tev = p.op("scalar", lambda h, ps=ps, sa=sa: h.copy(out=sa, in_=ps[:]),
                   after=[tmm, prevp.get(("stg", sl))], inc="act")
        prevp[tag % 2] = tev
        tst = p.dma("sync", out_ap, sa, after=[tev])
        prevp[("stg", sl)] = tst
        stores.append(tst)

    nblk = Nt // 512
    for f in range(NCH):
        for blk in range(nblk):
            mm_group(lambda c, f=f: qw[:, c * D + f * 128: c * D + (f + 1) * 128],
                     lambda c, blk=blk: xnfm[:, c * Nt + blk * 512: c * Nt + (blk + 1) * 512],
                     q_out[f * 128:(f + 1) * 128, blk * 512:(blk + 1) * 512],
                     t_qw, f * nblk + blk)
    for f in range(NCH):
        for blk in range(nblk):
            mm_group(lambda c, f=f: kw[:, c * D + f * 128: c * D + (f + 1) * 128],
                     lambda c, blk=blk: xnfm[:, c * Nt + blk * 512: c * Nt + (blk + 1) * 512],
                     k_out[f * 128:(f + 1) * 128, blk * 512:(blk + 1) * 512],
                     t_kw, f * nblk + blk)
    for i in range(ntile):
        for half in range(2):
            mm_group(lambda c, i=i: xnfm[:, c * Nt + i * 128: c * Nt + (i + 1) * 128],
                     lambda c, half=half: vw[:, c * D + half * 512: c * D + half * 512 + 512],
                     v_out[i * 128:(i + 1) * 128, half * 512:half * 512 + 512],
                     t_vw, i * 2 + half)
    p.op("gpsimd", None, after=stores)
    p.emit_and_close()
    return nc


class Arena:

    def __init__(self, p, name, cols_bf16):
        self.t = p.sb(name, [128, cols_bf16], BF16)
        self.cols = cols_bf16



def build_p4(Nt, NV, vc, KT=T):
    nc = bass.Bass("TRN2", target_bir_lowering=False,
                   detect_race_conditions=Prog.STRICT)
    NKC = KT // 128
    q_in = nc.dram_tensor("q_fm_b", [D, Nt], BF16, kind="ExternalInput").ap()
    k_in = nc.dram_tensor("k_fm_b", [D, KT], BF16, kind="ExternalInput").ap()
    v_in = nc.dram_tensor("v_tm_b", [KT, D], BF16, kind="ExternalInput").ap()
    swT_in = nc.dram_tensor("scale_wT2", [128, 16], F32, kind="ExternalInput").ap()
    xn_in = nc.dram_tensor("xn0_tm", [Nt, D], F32, kind="ExternalInput").ap()
    s0_in = nc.dram_tensor("s0_tm", [Nt, D], F32, kind="ExternalInput").ap()
    u_in = nc.dram_tensor("u_rows", [Nt, D], F32, kind="ExternalInput").ap()
    gw_in = nc.dram_tensor("gate_w", [D, D], F32, kind="ExternalInput").ap()
    ow_in = nc.dram_tensor("out_w", [D, D], F32, kind="ExternalInput").ap()
    vecs_in = nc.dram_tensor("vecs", [D, NV], F32, kind="ExternalInput").ap()
    rows_in = nc.dram_tensor("p4rows", [4, D], F32, kind="ExternalInput").ap()
    mgw_in = nc.dram_tensor("moe_gate_w", [D, 8], F32, kind="ExternalInput").ap()
    mrow_in = nc.dram_tensor("moe_rows", [2, 8], F32, kind="ExternalInput").ap()
    w1_in = nc.dram_tensor("w1", [8, D, 2048], F32, kind="ExternalInput").ap()
    b1_in = nc.dram_tensor("b1", [8, 2048], F32, kind="ExternalInput").ap()
    lng_in = nc.dram_tensor("ln_g", [8, 2048], F32, kind="ExternalInput").ap()
    lnb_in = nc.dram_tensor("ln_b", [8, 2048], F32, kind="ExternalInput").ap()
    w2_in = nc.dram_tensor("w2", [8, 2048, D], F32, kind="ExternalInput").ap()
    b2_in = nc.dram_tensor("b2", [8, D], F32, kind="ExternalInput").ap()
    x1_out = nc.dram_tensor("x1_tm", [Nt, D], F32, kind="ExternalOutput").ap()
    DBG = os.environ.get("P4DBG", "0") == "1"
    xr_scr = nc.dram_tensor("xr_scr", [Nt, D], F32,
                            kind="ExternalOutput" if DBG else "Internal").ap()
    if DBG:
        dbg_obf = nc.dram_tensor("dbg_obf", [D, Nt], BF16, kind="ExternalOutput").ap()
        dbg_msk = nc.dram_tensor("dbg_msk", [Nt, 8], F32, kind="ExternalOutput").ap()
        dbg_acc = nc.dram_tensor("dbg_acc", [Nt, D], BF16, kind="ExternalOutput").ap()
        dbg_hpre = nc.dram_tensor("dbg_hpre", [128, 2048], BF16, kind="ExternalOutput").ap()
        dbg_hpost = nc.dram_tensor("dbg_hpost", [128, 2048], BF16, kind="ExternalOutput").ap()
        dbg_h2 = nc.dram_tensor("dbg_h2", [128, 2048], BF16, kind="ExternalOutput").ap()
        dbg_o0 = nc.dram_tensor("dbg_o0", [128, D], F32, kind="ExternalOutput").ap()
        dbg_ps = nc.dram_tensor("dbg_ps", [128, 512], F32, kind="ExternalOutput").ap()
        dbg_msc = nc.dram_tensor("dbg_msc", [128, 1], F32, kind="ExternalOutput").ap()

    p = Prog(nc)
    ntile = Nt // 128
    nqsb = Nt // 512

    qfm = p.sb("qfm", [128, NCH * Nt], BF16)
    xrfm = p.sb("xrfm", [128, NCH * Nt], BF16)
    accb = p.sb("accb", [128, 8 * D], F32)
    vecs = p.sb("vecs_sb", [128, NCH * NV], F32)
    stats = p.sb("stats", [128, 32 * 32], F32)
    masks = p.sb("masks", [128, 8 * 16], F32)
    gbt = p.sb("gbt", [128, 2 * D], F32)
    xt1 = p.sb("xt1", [128, 2 * D], F32)
    xt2 = p.sb("xt2", [128, 2 * D], F32)
    xb1 = p.sb("xb1", [128, 2 * D], BF16)
    lob = p.sb("lob", [128, D], BF16)
    rowb = p.sb("rowb", [1, 512], F32)
    iotaE = p.sb("iotaE", [128, 8], F32)
    ident = p.sb("ident", [128, 128], BF16)
    iota2 = p.sb("iota2", [128, 128], F32)
    swT = p.sb("swT", [128, 16], BF16)
    scr = p.sb("scr", [128, 16], F32)
    rnkb = p.sb("rnkb", [128, 64], F32)
    srow16 = p.sb("srow16", [1, 128], F32)
    onesb = p.sb("onesb", [128, 2], BF16)
    ones1f = p.sb("ones1f", [1, 128], F32)
    halfrow = p.sb("halfrow", [1, 64], F32)
    rl = p.sb("rl", [1, 512], F32)
    negcb = p.sb("negcb", [128, 1], F32)
    mgbt = p.sb("mgbt", [128, 16], F32)
    mgw = p.sb("mgw", [128, NCH * 16], BF16)
    msct = p.sb("msct", [128, 8], F32)
    dbscr = p.sb("dbscr", [128, 512], F32) if DBG else None
    arena = Arena(p, "arena", 33792)

    AT = arena.t
    kbuf = AT[:, 0:KT]
    vbuf = AT[:, 8192:8192 + 65 * NKC]
    ktile = AT[:, 12352:12352 + D]
    ksqt = AT[:, 13376:13376 + D]
    ptb = AT[:, 14400:14400 + 512]
    oev = AT[:, 14912:14912 + 1024].bitcast(F32)[0:64, :]
    maxcol = AT[:, 16384:16512].bitcast(F32)
    obf = AT[:, 17408:17408 + NCH * Nt]
    owb = AT[:, 0:NCH * D]
    gwb = AT[:, NCH * D:2 * NCH * D]
    gsig = AT[:, 16896:16896 + 512]
    w1buf = AT[:, 0:NCH * 2048]
    w2buf = AT[:, 16384:16384 + 16 * D]
    b1bc = qfm[:, 4096:4096 + 3 * 2048]
    hti = qfm[:, 10240:10240 + 2048]
    b2bc = qfm[:, 12288:14336].bitcast(F32)
    h2b = None

    psA = p.psum("psA", [128, 512], F32)
    psB = p.psum("psB", [128, 512], F32)
    psPV = p.psum("psPV", [128, 512], F32)
    psC = p.psum("psC", [128, 512], F32)
    psD = p.psum("psD", [128, 512], F32)
    psE = p.psum("psE", [128, 512], F32)
    tid = emit_identity(p, ident, iota2)
    tr = Transposer(p, tid, ident)

    tok_v = [p.dma("gpsimd", vecs[:, c * NV:(c + 1) * NV], vecs_in[c * 128:(c + 1) * 128, :])
             for c in range(NCH)]
    t_q = [p.dma("sync", qfm[:, c * Nt:(c + 1) * Nt], q_in[c * 128:(c + 1) * 128, :])
           for c in range(NCH)]
    t_sw = p.dma("gpsimd", swT[:], swT_in[:])
    tz = [p.op("gpsimd", lambda h: h.memset(onesb[:], 0.0), inc="gp"),
          p.op("gpsimd", lambda h: h.memset(onesb[0:64, 0:1], 1.0), inc="gp"),
          p.op("gpsimd", lambda h: h.memset(onesb[64:128, 1:2], 1.0), inc="gp"),
          p.op("gpsimd", lambda h: h.memset(ones1f[:], 1.0), inc="gp"),
          p.op("gpsimd", lambda h: h.memset(halfrow[:], 0.25), inc="gp")]
    t_iE = p.op("gpsimd", lambda h: h.iota(iotaE[:], pattern=[[1, 8]], base=0,
                                           channel_multiplier=0,
                                           allow_small_or_imprecise_dtypes=True), inc="gp")
    tzt = tz[-1]

    bcst = {"i": 0}

    def bcast_piece(src_row_dram, dst_sb, W, after=()):
        i = bcst["i"]
        bcst["i"] += 1
        ps = psC if i % 2 == 0 else psD
        trow = p.dma("sync", rowb[0:1, 0:W], src_row_dram,
                     after=[bcst.get("row")] + list(after))
        tm_ = p.op("tensor", lambda h, W=W, ps=ps: h.matmul(
            ps[0:128, 0:W], ones1f[:], rowb[0:1, 0:W], start=True, stop=True),
            after=[trow, tzt, bcst.get(("ps", ps is psD))], inc="pe")
        tev = p.op("vector", lambda h, W=W, ps=ps, dst_sb=dst_sb: h.tensor_copy(
            dst_sb, ps[0:128, 0:W]), after=[tm_], inc="dve")
        bcst["row"] = tev
        bcst[("ps", ps is psD)] = tev
        return tev

    def pmax_scalar(col_f32, out_col, after):
        td = p.dma("sync", srow16[0:1, 0:128], col_f32, after=after)
        return p.op("vector", lambda h: h.reduce_max(out_col, srow16[0:1, 0:128],
                                                     axis=AX.X), after=[td], inc="dve")

    tqp = {}
    lastq = None
    for hd in range(16):
        par, ch = hd % 2, hd // 2
        base = par * 64
        for blk in range(nqsb):
            sl = slice(ch * Nt + blk * 512, ch * Nt + (blk + 1) * 512)
            tm_ = p.op("tensor", lambda h, sl=sl, base=base, hd=hd: h.matmul(
                psE[0:1, 0:512], swT[base:base + 64, hd:hd + 1],
                qfm[base:base + 64, sl], start=True, stop=True),
                after=t_q + [t_sw, tqp.get("psE")], inc="pe")
            tsg = p.op("scalar", lambda h: h.activation(
                out=rl[0:1, 0:512], in_=psE[0:1, 0:512], func=AF.Sigmoid),
                after=[tm_, tqp.get("rl")], inc="act")
            tqp["psE"] = tsg
            tb_ = p.op("tensor", lambda h, base=base: h.matmul(
                psD[base:base + 64, 0:512], halfrow[:], rl[0:1, 0:512],
                start=True, stop=True),
                after=[tsg, tzt, tqp.get("psD")], inc="pe")
            tmul = p.op("vector", lambda h, sl=sl, base=base: h.tensor_tensor(
                qfm[base:base + 64, sl], qfm[base:base + 64, sl],
                psD[base:base + 64, 0:512], ALU.mult), after=[tb_], inc="dve")
            tqp["rl"] = tmul
            tqp["psD"] = tmul
            lastq = tmul
    qp_done = [lastq]

    prevm = {}
    qmax_toks = []
    for i in range(ntile):
        for c in range(NCH):
            tq1 = p.op("scalar", lambda h, i=i, c=c: h.activation(
                out=ksqt[:, 0:128], in_=qfm[:, c * Nt + i * 128: c * Nt + (i + 1) * 128],
                func=AF.Square), after=qp_done + [prevm.get("ksqt")], inc="act")
            tm_ = p.op("tensor", lambda h, c=c: h.matmul(
                psE[0:128, 2 * c:2 * c + 2], ksqt[:, 0:128], onesb[:],
                start=True, stop=True),
                after=[tq1, tzt, prevm.get("psEq"), tqp.get("psE")], inc="pe")
            prevm["ksqt"] = tm_
        tmm = ("pe", p.cnt["pe"])
        trd = p.op("vector", lambda h, i=i: h.reduce_max(
            maxcol[:, i:i + 1], psE[0:128, 0:16], axis=AX.X), after=[tmm], inc="dve")
        prevm["psEq"] = trd
        qmax_toks.append(trd)
    tqm = p.op("vector", lambda h: h.reduce_max(scr[:, 0:1], maxcol[:, 0:ntile],
                                                axis=AX.X), after=qmax_toks, inc="dve")
    tqs = pmax_scalar(scr[:, 0:1], scr[0:1, 1:2], [tqm])

    prevk = {}
    kmax_toks = []
    for kt in range(NKC):
        tld = p.dma("scalar", ktile,
                    k_in.rearrange("(c p) t -> p c t", c=NCH)[:, :, kt * 128:(kt + 1) * 128],
                    after=[prevk.get("ktile")])
        tsq = p.op("scalar", lambda h: h.activation(out=ksqt, in_=ktile, func=AF.Square),
                   after=[tld, prevm.get("ksqt"), prevk.get("ksq")], inc="act")
        prevm["ksqt"] = tsq
        for c in range(NCH):
            p.op("tensor", lambda h, c=c: h.matmul(
                psE[0:128, 2 * c:2 * c + 2], ksqt[:, c * 128:(c + 1) * 128], onesb[:],
                start=True, stop=True),
                 after=[tsq, prevk.get("psEk"),
                        prevm.get("psEq") if kt == 0 else None],
                 inc="pe" if c == NCH - 1 else None)
        tmm = ("pe", p.cnt["pe"])
        trd = p.op("vector", lambda h, kt=kt: h.reduce_max(
            maxcol[:, kt:kt + 1], psE[0:128, 0:16], axis=AX.X), after=[tmm], inc="dve")
        prevk["psEk"] = trd
        prevk["ktile"] = tsq
        prevk["ksq"] = trd
        kmax_toks.append(trd)
    tkm = p.op("vector", lambda h: h.reduce_max(scr[:, 2:3], maxcol[:, 0:NKC],
                                                axis=AX.X), after=kmax_toks, inc="dve")
    tks = pmax_scalar(scr[:, 2:3], scr[0:1, 3:4], [tkm])
    tc1 = p.op("vector", lambda h: h.tensor_tensor(scr[0:1, 4:5], scr[0:1, 1:2],
                                                   scr[0:1, 3:4], ALU.mult),
               after=[tqs, tks], inc="dve")
    tc2 = p.op("scalar", lambda h: h.activation(out=scr[0:1, 5:6], in_=scr[0:1, 4:5],
                                                func=AF.Sqrt), after=[tc1], inc="act")
    tc3 = p.op("vector", lambda h: h.tensor_scalar_mul(rl[0:1, 0:1], scr[0:1, 5:6], -1.0),
               after=[tc2], inc="dve")
    tc4 = p.op("tensor", lambda h: h.matmul(psC[0:128, 0:1], ones1f[:], rl[0:1, 0:1],
                                            start=True, stop=True),
               after=[tc3, tzt], inc="pe")
    tc5 = p.op("vector", lambda h: h.tensor_copy(negcb[:], psC[0:128, 0:1]),
               after=[tc4], inc="dve")

    ah = {}
    for hd in range(16):
        par, ch = hd % 2, hd // 2
        base = par * 64
        tkl = p.dma("sync", kbuf[base:base + 64, :], k_in[hd * 64:(hd + 1) * 64, :],
                    after=[ah.get(("kbuf", par)), tks])
        tvm = p.op("gpsimd", lambda h: h.memset(vbuf[:], 1.0),
                   after=[ah.get("vbuf")], inc="gp")
        tvl = p.dma("scalar",
                    vbuf.rearrange("p (kc j) -> p kc j", j=65)[:, :, 0:64],
                    v_in.rearrange("(kc p) d -> p kc d", p=128)[:, :, hd * 64:(hd + 1) * 64],
                    after=[tvm])
        for qsb in range(nqsb):
            qsl = slice(ch * Nt + qsb * 512, ch * Nt + (qsb + 1) * 512)
            for kc in range(NKC):
                psS = psA if kc % 2 == 0 else psB
                tms = p.op("tensor", lambda h, kc=kc, qsl=qsl, base=base, psS=psS: h.matmul(
                    psS[0:128, 0:512], kbuf[base:base + 64, kc * 128:(kc + 1) * 128],
                    qfm[base:base + 64, qsl], start=True, stop=True),
                    after=[tkl, ah.get(("psS", kc % 2))] + qp_done, inc="pe")
                texp = p.op("scalar", lambda h, psS=psS: h.activation(
                    out=ptb, in_=psS[0:128, 0:512], func=AF.Exp, bias=negcb[:, 0:1]),
                    after=[tms, tc5, ah.get("pt")], inc="act")
                ah[("psS", kc % 2)] = texp
                tpv = p.op("tensor", lambda h, kc=kc: h.matmul(
                    psPV[0:65, 0:512], vbuf[:, kc * 65:(kc + 1) * 65], ptb,
                    start=(kc == 0), stop=(kc == NKC - 1)),
                    after=[texp, tvl] + ([ah.get("psPV")] if kc == 0 else []), inc="pe")
                ah["pt"] = tpv
            tpv_done = ("pe", p.cnt["pe"])
            trc = p.op("vector", lambda h: h.reciprocal(rl[0:1, 0:512], psPV[64:65, 0:512]),
                       after=[tpv_done, ah.get("rl2")], inc="dve")
            tbc = p.op("tensor", lambda h: h.matmul(
                psC[0:64, 0:512], ones1f[0:1, 0:64], rl[0:1, 0:512],
                start=True, stop=True), after=[trc, ah.get("psCb"), tc5], inc="pe")
            tev = p.op("scalar", lambda h: h.copy(out=oev, in_=psPV[0:64, 0:512]),
                       after=[tpv_done, ah.get("oev")], inc="act")
            tmul = p.op("vector", lambda h, qsl=qsl, base=base: h.tensor_tensor(
                obf[base:base + 64, qsl], oev, psC[0:64, 0:512], ALU.mult),
                after=[tbc, tev], inc="dve")
            ah["psPV"] = tmul
            ah["rl2"] = tmul
            ah["psCb"] = tmul
            ah["oev"] = tmul
            ah[("attn", hd, qsb)] = tmul
        ah[("kbuf", par)] = ("pe", p.cnt["pe"])
        ah["vbuf"] = ("pe", p.cnt["pe"])
    attn_done = [ah[("attn", hd, qsb)] for hd in range(16) for qsb in range(nqsb)]
    if DBG:
        for c in range(NCH):
            p.dma("sync", dbg_obf[c * 128:(c + 1) * 128, :], obf[:, c * Nt:(c + 1) * Nt],
                  after=attn_done)

    t_ow = [p.dma("gpsimd", owb[:, c * D:(c + 1) * D], ow_in[c * 128:(c + 1) * 128, :],
                  after=attn_done if c == 0 else ()) for c in range(NCH)]
    t_gw = [p.dma("gpsimd", gwb[:, c * D:(c + 1) * D], gw_in[c * 128:(c + 1) * 128, :],
                  after=() if c else attn_done) for c in range(NCH)]
    t_mg = [p.dma("gpsimd", mgw[:, c * 16:c * 16 + 8], mgw_in[c * 128:(c + 1) * 128, :])
            for c in range(NCH)]
    prevlo = None
    for c in range(NCH):
        tld = p.dma("sync", xt1[:, 0:8], mgw_in[c * 128:(c + 1) * 128, :],
                    after=[prevlo])
        tsub = p.op("vector", lambda h, c=c: h.scalar_tensor_tensor(
            out=xt1[:, 8:16], in0=mgw[:, c * 16:c * 16 + 8], scalar=-1.0,
            in1=xt1[:, 0:8], op0=ALU.mult, op1=ALU.add),
            after=[tld, t_mg[c]], inc="dve")
        prevlo = p.op("scalar", lambda h, c=c: h.copy(
            out=mgw[:, c * 16 + 8:c * 16 + 16], in_=xt1[:, 8:16]),
            after=[tsub], inc="act")
    t_mglo = [prevlo]
    tgb = bcast_piece(mrow_in[0:1, 0:8], mgbt[:, 0:8], 8, after=attn_done)
    trs = bcast_piece(mrow_in[1:2, 0:8], mgbt[:, 8:16], 8)
    tn1 = [bcast_piece(rows_in[0:1, 0:512], gbt[:, 0:512], 512),
           bcast_piece(rows_in[0:1, 512:1024], gbt[:, 512:1024], 512),
           bcast_piece(rows_in[1:2, 0:512], gbt[:, D:D + 512], 512),
           bcast_piece(rows_in[1:2, 512:1024], gbt[:, D + 512:2 * D], 512)]

    bg = {}
    og_done = []
    for f in range(NCH):
        for blk in range(nqsb):
            ps = psA if (f * nqsb + blk) % 2 == 0 else psB
            for c in range(NCH):
                aft = []
                if c == 0:
                    aft = t_gw + attn_done + [bg.get(("ps", (f * nqsb + blk) % 2))]
                p.op("tensor", lambda h, f=f, c=c, blk=blk, ps=ps: h.matmul(
                    ps[:], gwb[:, c * D + f * 128: c * D + (f + 1) * 128],
                    obf[:, c * Nt + blk * 512: c * Nt + (blk + 1) * 512],
                    start=(c == 0), stop=(c == NCH - 1)),
                    after=aft, inc="pe" if c == NCH - 1 else None)
            tmm = ("pe", p.cnt["pe"])
            tsig = p.op("scalar", lambda h, f=f, ps=ps: h.activation(
                out=gsig, in_=ps[:], func=AF.Sigmoid,
                bias=vecs[:, f * NV + vc["gate_b"]: f * NV + vc["gate_b"] + 1]),
                after=[tmm, bg.get("gsig"), tok_v[f]], inc="act")
            tog = p.op("vector", lambda h, f=f, blk=blk: h.tensor_tensor(
                qfm[:, f * Nt + blk * 512: f * Nt + (blk + 1) * 512],
                obf[:, f * Nt + blk * 512: f * Nt + (blk + 1) * 512], gsig, ALU.mult),
                after=[tsig] + qmax_toks, inc="dve")
            bg[("ps", (f * nqsb + blk) % 2)] = tog
            bg["gsig"] = tog
            og_done.append(tog)

    bt = {("xn", 0): t_mglo[0]}
    router_done = []
    xr_stored = {}
    for i in range(ntile):
        sl2 = i % 2
        xn_t = xt1[:, sl2 * D:(sl2 + 1) * D]
        x_t = xt2[:, sl2 * D:(sl2 + 1) * D]
        xbh = xb1[:, 0:D]
        xbl = xb1[:, D:2 * D]
        txn = p.dma("sync", xn_t, xn_in[i * 128:(i + 1) * 128, :],
                    after=[bt.get(("xn", sl2))])
        tadds = []
        for half in range(2):
            ps = psA if half == 0 else psB
            for c in range(NCH):
                aft = []
                if c == 0:
                    aft = og_done + t_ow + [bt.get(("ps", half))]
                p.op("tensor", lambda h, c=c, i=i, half=half, ps=ps: h.matmul(
                    ps[:], qfm[:, c * Nt + i * 128: c * Nt + (i + 1) * 128],
                    owb[:, c * D + half * 512: c * D + half * 512 + 512],
                    start=(c == 0), stop=(c == NCH - 1)),
                    after=aft, inc="pe" if c == NCH - 1 else None)
            tmm = ("pe", p.cnt["pe"])
            tadd = p.op("vector", lambda h, ps=ps, half=half, x_t=x_t, xn_t=xn_t: h.tensor_tensor(
                x_t[:, half * 512:(half + 1) * 512], ps[:],
                xn_t[:, half * 512:(half + 1) * 512], ALU.add),
                after=[tmm, txn, bt.get(("x", sl2))], inc="dve")
            bt[("ps", half)] = tadd
            tadds.append(tadd)
        tln = tm_ln_emit(p, stats, i, x_t, x_t, D, after=tadds, kind="ln",
                         gb=(gbt[:, 0:D], gbt[:, D:2 * D], tn1[-1]))
        txr = p.dma("scalar", xr_scr[i * 128:(i + 1) * 128, :], x_t, after=[tln])
        xr_stored[i] = txr
        tch = p.op("scalar", lambda h, x_t=x_t: h.copy(out=xbh, in_=x_t),
                   after=[tln, bt.get("xbh")], inc="act")
        tlo = p.op("vector", lambda h, x_t=x_t: h.scalar_tensor_tensor(
            out=x_t, in0=xbh, scalar=-1.0, in1=x_t, op0=ALU.mult, op1=ALU.add),
            after=[tch, txr], inc="dve")
        tcl = p.op("scalar", lambda h, x_t=x_t: h.copy(out=xbl, in_=x_t),
                   after=[tlo, bt.get("xbl")], inc="act")
        trh = []
        trl = []
        for c in range(NCH):
            trh.append(tr.transpose(xbh[:, c * 128:(c + 1) * 128],
                                    xrfm[:, c * Nt + i * 128: c * Nt + (i + 1) * 128],
                                    after=[tch]))
            trl.append(tr.transpose(xbl[:, c * 128:(c + 1) * 128],
                                    lob[:, c * 128:(c + 1) * 128],
                                    after=[tcl, bt.get(("lob", c))]))
        for j, (xsrc, xtok, wcol) in enumerate((("hi", trh, 0), ("lo", trl, 0),
                                                ("hi", trh, 8))):
            for c in range(NCH):
                lhs = (xrfm[:, c * Nt + i * 128: c * Nt + (i + 1) * 128]
                       if xsrc == "hi" else lob[:, c * 128:(c + 1) * 128])
                aft = list(xtok)
                if c == 0 and j == 0:
                    aft += t_mglo + t_mg + [bt.get("psE2")]
                p.op("tensor", lambda h, lhs=lhs, wcol=wcol, c=c, j=j: h.matmul(
                    psE[0:128, 0:8], lhs, mgw[:, c * 16 + wcol: c * 16 + wcol + 8],
                    start=(c == 0 and j == 0), stop=(c == NCH - 1 and j == 2)),
                    after=aft, inc="pe" if (c == NCH - 1 and j == 2) else None)
        tmm = ("pe", p.cnt["pe"])
        for c in range(NCH):
            bt[("lob", c)] = tmm
        lg = scr
        t0 = p.op("vector", lambda h: h.tensor_tensor(lg[:, 0:8], psE[0:128, 0:8],
                                                      mgbt[:, 0:8], ALU.add),
                  after=[tmm, tgb, bt.get("lg")], inc="dve")
        bt["psE2"] = t0
        t2 = p.op("vector", lambda h: h.tensor_tensor(
            rnkb[:, 0:64].rearrange("p (e a) -> p e a", e=8),
            lg[:, 0:8].unsqueeze(1).broadcast_to([128, 8, 8]),
            lg[:, 0:8].unsqueeze(2).broadcast_to([128, 8, 8]),
            ALU.is_gt), after=[t0, bt.get("rnkb")], inc="dve")
        t3 = p.op("vector", lambda h: h.reduce_sum(
            lg[:, 8:16], rnkb[:, 0:64].rearrange("p (e a) -> p e a", e=8), axis=AX.X),
            after=[t2], inc="dve")
        t4 = p.op("vector", lambda h: h.tensor_single_scalar(lg[:, 8:16], lg[:, 8:16],
                                                             1.5, ALU.is_lt),
                  after=[t3], inc="dve")
        t5 = p.op("vector", lambda h: h.tensor_tensor(rnkb[:, 0:8], lg[:, 8:16],
                                                      iotaE[:], ALU.mult),
                  after=[t4, t_iE], inc="dve")
        t6 = p.op("vector", lambda h: h.reduce_max(lg[:, 8:9], rnkb[:, 0:8], axis=AX.X),
                  after=[t5], inc="dve")
        t7 = p.op("vector", lambda h, i=i: h.tensor_tensor(
            masks[:, 8 * i:8 * i + 8], lg[:, 8:9].broadcast_to([128, 8]), iotaE[:],
            ALU.is_equal), after=[t6], inc="dve")
        bt["rnkb"] = t7
        bt["lg"] = t7
        bt[("xn", sl2)] = t7
        bt[("x", sl2)] = t7
        bt["xbh"] = trh[-1]
        bt["xbl"] = trl[-1]
        router_done.append(t7)

    if DBG:
        for i in range(ntile):
            p.dma("sync", dbg_msk[i * 128:(i + 1) * 128, :], masks[:, 8 * i:8 * i + 8],
                  after=router_done)
    tn2 = [bcast_piece(rows_in[2:3, 0:512], gbt[:, 0:512], 512, after=router_done),
           bcast_piece(rows_in[2:3, 512:1024], gbt[:, 512:1024], 512),
           bcast_piece(rows_in[3:4, 0:512], gbt[:, D:D + 512], 512),
           bcast_piece(rows_in[3:4, 512:1024], gbt[:, D + 512:2 * D], 512)]
    dd = {}
    x1_stores = []
    nhalf = ntile // 2
    for H in range(2):
        tiles = list(range(H * nhalf, (H + 1) * nhalf))
        tacc = {}
        for j, i in enumerate(tiles):
            sl2 = i % 2
            tld = p.dma("sync", xt1[:, sl2 * D:(sl2 + 1) * D],
                        xr_scr[i * 128:(i + 1) * 128, :],
                        after=[xr_stored[i], dd.get(("xt1", sl2))] +
                              (router_done if (H == 0 and j < 2) else []))
            tcp = p.op("scalar", lambda h, j=j, sl2=sl2: h.activation(
                out=accb[:, j * D:(j + 1) * D], in_=xt1[:, sl2 * D:(sl2 + 1) * D],
                func=AF.Copy, scale=2.0),
                after=[tld, dd.get(("accb", H, j))], inc="act")
            dd[("xt1", sl2)] = tcp
            tacc[j] = tcp
        for e in range(8):
            t_w1 = []
            for c in range(NCH):
                aft = []
                if c == 0:
                    aft = [dd.get("w1r")] + (router_done if (H == 0 and e == 0) else [])
                t_w1.append(p.dma("gpsimd", w1buf[:, c * 2048:(c + 1) * 2048],
                                  w1_in[e, c * 128:(c + 1) * 128, :], after=aft))
            t_w2 = []
            for fc in range(16):
                aft = []
                if fc == 0:
                    aft = [dd.get("w2r")] + (router_done if (H == 0 and e == 0) else []) + og_done
                t_w2.append(p.dma("gpsimd", w2buf[:, fc * D:(fc + 1) * D],
                                  w2_in[e, fc * 128:(fc + 1) * 128, :], after=aft))
            bguard = [dd.get("b1r")] + (router_done if (H == 0 and e == 0) else []) + og_done
            tb1 = [bcast_piece(b1_in[e:e + 1, b0:b0 + 512], b1bc[:, b0:b0 + 512], 512,
                               after=bguard if b0 == 0 else ())
                   for b0 in range(0, 2048, 512)]
            tlg = [bcast_piece(lng_in[e:e + 1, b0:b0 + 512],
                               b1bc[:, 2048 + b0:2048 + b0 + 512], 512)
                   for b0 in range(0, 2048, 512)]
            tlb = [bcast_piece(lnb_in[e:e + 1, b0:b0 + 512],
                               b1bc[:, 4096 + b0:4096 + b0 + 512], 512)
                   for b0 in range(0, 2048, 512)]
            tb2 = [bcast_piece(b2_in[e:e + 1, b0:b0 + 512], b2bc[:, b0:b0 + 512], 512)
                   for b0 in range(0, 1024, 512)]
            for j, i in enumerate(tiles):
                tms = p.op("vector", lambda h, i=i, e=e: h.tensor_tensor(
                    msct[:, e:e + 1], masks[:, 8 * i + e:8 * i + e + 1],
                    mgbt[:, 8 + e:9 + e], ALU.mult),
                    after=[router_done[i], trs, dd.get("msc")], inc="dve")
                for fb in range(4):
                    ps = psA if fb % 2 == 0 else psB
                    for c in range(NCH):
                        aft = []
                        if c == 0:
                            aft = t_w1 + [dd.get(("psh", fb % 2))]
                        p.op("tensor", lambda h, c=c, i=i, fb=fb, ps=ps: h.matmul(
                            ps[:], xrfm[:, c * Nt + i * 128: c * Nt + (i + 1) * 128],
                            w1buf[:, c * 2048 + fb * 512: c * 2048 + fb * 512 + 512],
                            start=(c == 0), stop=(c == NCH - 1)),
                            after=aft, inc="pe" if c == NCH - 1 else None)
                    tmm = ("pe", p.cnt["pe"])
                    if (DBG and H == 0 and e == 0 and fb == 0
                            and j == int(os.environ.get("P4DBGJ", "0"))):
                        tdps = p.op("vector", lambda h, ps=ps: h.tensor_copy(
                            dbscr[:], ps[:]), after=[tmm], inc="dve")
                        p.dma("sync", dbg_ps[:, 0:512], dbscr[:], after=[tdps])
                    tev = p.op("vector", lambda h, fb=fb, ps=ps: h.tensor_tensor(
                        hti[:, fb * 512:(fb + 1) * 512], ps[:],
                        b1bc[:, fb * 512:(fb + 1) * 512], ALU.add),
                        after=[tmm, tb1[fb], dd.get("hti")], inc="dve")
                    dd[("psh", fb % 2)] = tev
                tpre = None
                if DBG and H == 0 and e == 0 and j == int(os.environ.get('P4DBGJ', '0')):
                    tpre = p.dma("sync", dbg_hpre[:], hti[:, 0:2048],
                                 after=[dd[("psh", 0)], dd[("psh", 1)]])
                tln = tm_ln_emit(p, stats, 16 + (i % 8), hti[:, 0:2048], hti[:, 0:2048],
                                 2048, after=[dd[("psh", 0)], dd[("psh", 1)], tpre],
                                 kind="ln",
                                 gb=(b1bc[:, 2048:4096], b1bc[:, 4096:6144], tlb[-1]))
                tpost = None
                if DBG and H == 0 and e == 0 and j == int(os.environ.get('P4DBGJ', '0')):
                    tpost = p.dma("sync", dbg_hpost[:], hti[:, 0:2048], after=[tln])
                h2b = qfm[:, 2048:4096]
                tgel = p.op("scalar", lambda h, h2b=h2b: h.activation(
                    out=h2b, in_=hti[:, 0:2048], func=AF.Gelu),
                    after=[tln, dd.get("h2b"), tpost], inc="act")
                if DBG and H == 0 and e == 0 and j == int(os.environ.get('P4DBGJ', '0')):
                    tg2 = p.dma("sync", dbg_h2[:], h2b, after=[tgel])
                    tgel = tg2
                h2f = qfm
                ttr = []
                for fc in range(16):
                    ttr.append(tr.transpose(
                        h2b[:, fc * 128:(fc + 1) * 128],
                        h2f[:, fc * 128:(fc + 1) * 128],
                        after=[tgel, dd.get("h2f")] +
                              (router_done if (H == 0 and e == 0 and j == 0) else [])))
                dd["h2f"] = ttr[-1]
                dd["hti"] = tgel
                for half in range(2):
                    ps = psPV if half == 0 else psE
                    for fc in range(16):
                        aft = list(ttr)
                        if fc == 0:
                            aft += t_w2 + [dd.get(("pso", half))]
                        p.op("tensor", lambda h, fc=fc, half=half, ps=ps: h.matmul(
                            ps[:], h2f[:, fc * 128:(fc + 1) * 128],
                            w2buf[:, fc * D + half * 512: fc * D + half * 512 + 512],
                            start=(fc == 0), stop=(fc == 15)),
                            after=aft, inc="pe" if fc == 15 else None)
                    tmm = ("pe", p.cnt["pe"])
                    sl2 = i % 2
                    ttmp = p.op("vector", lambda h, half=half, ps=ps, sl2=sl2: h.tensor_tensor(
                        xt2[:, sl2 * D + half * 512: sl2 * D + half * 512 + 512],
                        ps[:], b2bc[:, half * 512:(half + 1) * 512], ALU.add),
                        after=[tmm, tb2[half], dd.get(("xt2", sl2, half))], inc="dve")
                    if DBG and H == 0 and e == 0 and j == int(os.environ.get('P4DBGJ', '0')):
                        p.dma("sync", dbg_o0[:, half * 512:(half + 1) * 512],
                              xt2[:, sl2 * D + half * 512: sl2 * D + half * 512 + 512],
                              after=[ttmp])
                        if half == 0:
                            p.dma("sync", dbg_msc[:, 0:1], msct[:, 0:1], after=[tms])
                    tupd = p.op("vector", lambda h, j=j, half=half, sl2=sl2, e=e: h.scalar_tensor_tensor(
                        out=accb[:, j * D + half * 512: j * D + half * 512 + 512],
                        in0=xt2[:, sl2 * D + half * 512: sl2 * D + half * 512 + 512],
                        scalar=msct[:, e:e + 1],
                        in1=accb[:, j * D + half * 512: j * D + half * 512 + 512],
                        op0=ALU.mult, op1=ALU.add),
                        after=[ttmp, tms, tacc[j]], inc="dve")
                    dd[("pso", half)] = tupd
                    dd[("xt2", sl2, half)] = tupd
                    dd[("acc", j, half)] = tupd
                dd["msc"] = dd[("acc", j, 1)]
            dd["w1r"] = ("pe", p.cnt["pe"])
            dd["w2r"] = ("pe", p.cnt["pe"])
            dd["b1r"] = dd[("acc", len(tiles) - 1, 1)]
            dd["h2b"] = ("pe", p.cnt["pe"])
        for j, i in enumerate(tiles):
            sl2 = i % 2
            x_t = xt2[:, sl2 * D:(sl2 + 1) * D]
            tdbg = None
            if DBG:
                tdbg = p.dma("scalar", dbg_acc[i * 128:(i + 1) * 128, :],
                             accb[:, j * D:(j + 1) * D],
                             after=[dd[("acc", j, 0)], dd[("acc", j, 1)]])
            tln = tm_ln_emit(p, stats, 24 + (i % 4), accb[:, j * D:(j + 1) * D],
                             x_t, D,
                             after=[dd[("acc", j, 0)], dd[("acc", j, 1)],
                                    dd.get(("xt2", sl2, 0)), dd.get(("xt2", sl2, 1))],
                             kind="ln", gb=(gbt[:, 0:D], gbt[:, D:2 * D], tn2[-1]))
            tsi = p.op("scalar", lambda h, x_t=x_t: h.activation(
                out=x_t, in_=x_t, func=AF.Silu), after=[tln], inc="act")
            tld = p.dma("sync", xt1[:, sl2 * D:(sl2 + 1) * D],
                        s0_in[i * 128:(i + 1) * 128, :], after=[dd.get(("xt1", sl2))])
            tmul = p.op("vector", lambda h, x_t=x_t, sl2=sl2: h.tensor_tensor(
                x_t, x_t, xt1[:, sl2 * D:(sl2 + 1) * D], ALU.mult),
                after=[tsi, tld], inc="dve")
            tld2 = p.dma("sync", xt1[:, sl2 * D:(sl2 + 1) * D],
                         u_in[i * 128:(i + 1) * 128, :], after=[tmul])
            tadd = p.op("vector", lambda h, x_t=x_t, sl2=sl2: h.tensor_tensor(
                x_t, x_t, xt1[:, sl2 * D:(sl2 + 1) * D], ALU.add),
                after=[tld2], inc="dve")
            tst = p.dma("sync", x1_out[i * 128:(i + 1) * 128, :], x_t, after=[tadd])
            x1_stores.append(tst)
            dd[("xt1", sl2)] = tadd
            dd[("xt2", sl2, 0)] = tst
            dd[("xt2", sl2, 1)] = tst
            dd[("accb", (H + 1) % 2, j)] = tdbg or tln
    p.op("gpsimd", None, after=x1_stores)
    p.emit_and_close()
    return nc



def build_p5(Nt, NV, vc):
    nc = bass.Bass("TRN2", target_bir_lowering=False,
                   detect_race_conditions=Prog.STRICT)
    NtE = Nt + 128
    ntile = NtE // 128
    x1_in = nc.dram_tensor("x1_ext", [NtE, D], F32, kind="ExternalInput").ap()
    hv_in = nc.dram_tensor("halo_valid", [1, 1], F32, kind="ExternalInput").ap()
    wk_in = nc.dram_tensor("WkT", [D, D], F32, kind="ExternalInput").ap()
    wv_in = nc.dram_tensor("WvT", [D, D], F32, kind="ExternalInput").ap()
    wr_in = nc.dram_tensor("WrT", [D, D], F32, kind="ExternalInput").ap()
    vecs_in = nc.dram_tensor("vecs", [D, NV], F32, kind="ExternalInput").ap()
    rows_in = nc.dram_tensor("p5rows", [4, D], F32, kind="ExternalInput").ap()
    k_out = nc.dram_tensor("k1_fm", [D, Nt], F32, kind="ExternalOutput").ap()
    v_out = nc.dram_tensor("v1_fm", [D, Nt], F32, kind="ExternalOutput").ap()
    sr_out = nc.dram_tensor("sr_fm", [D, Nt], F32, kind="ExternalOutput").ap()
    xp_out = nc.dram_tensor("xp_tm", [Nt, D], F32, kind="ExternalOutput").ap()
    xn1_out = nc.dram_tensor("xn1_tm", [Nt, D], F32, kind="ExternalOutput").ap()

    p = Prog(nc)
    wkb = p.sb("wkb", [128, NCH * D], BF16)
    wvb = p.sb("wvb", [128, NCH * D], BF16)
    wrb = p.sb("wrb", [128, NCH * D], BF16)
    hfm = p.sb("hfm", [128, NCH * NtE], BF16)
    mixb = p.sb("mixb", [128, NCH * Nt], BF16)
    vecs = p.sb("vecs_sb", [128, NCH * NV], F32)
    gbt = p.sb("gbt", [128, 4 * D], F32)
    stats = p.sb("stats", [128, 32 * 3 * ntile], F32)
    xt = p.sb("xt", [128, 2 * D], F32)
    xnt = p.sb("xnt", [128, 2 * D], F32)
    xpt = p.sb("xpt", [128, 2 * D], F32)
    hht = p.sb("hht", [128, 2 * D], F32)
    hbt = p.sb("hbt", [128, 2 * D], BF16)
    stg = p.sb("stg", [128, 4 * 512], F32)
    rowb = p.sb("rowb", [1, 512], F32)
    ones1f = p.sb("ones1f", [1, 128], F32)
    hvb = p.sb("hvb", [128, 1], F32)
    omx = p.sb("omx", [128, NCH * 3], F32)
    ident = p.sb("ident", [128, 128], BF16)
    iota2 = p.sb("iota2", [128, 128], F32)

    psA = p.psum("psA", [128, 512], F32)
    psB = p.psum("psB", [128, 512], F32)
    psC = p.psum("psC", [128, 512], F32)
    psD = p.psum("psD", [128, 512], F32)
    tid = emit_identity(p, ident, iota2)
    tr = Transposer(p, tid, ident)

    tok_v = [p.dma("gpsimd", vecs[:, c * NV:(c + 1) * NV], vecs_in[c * 128:(c + 1) * 128, :])
             for c in range(NCH)]
    t_wk = [p.dma("gpsimd", wkb[:, c * D:(c + 1) * D], wk_in[c * 128:(c + 1) * 128, :])
            for c in range(NCH)]
    t_wv = [p.dma("gpsimd", wvb[:, c * D:(c + 1) * D], wv_in[c * 128:(c + 1) * 128, :])
            for c in range(NCH)]
    t_wr = [p.dma("gpsimd", wrb[:, c * D:(c + 1) * D], wr_in[c * 128:(c + 1) * 128, :])
            for c in range(NCH)]
    tz = p.op("gpsimd", lambda h: h.memset(ones1f[:], 1.0), inc="gp")
    t_om = []
    for c in range(NCH):
        for j, nm in enumerate(("tmk", "tmv", "tmr")):
            col = vecs[:, c * NV + vc[nm]: c * NV + vc[nm] + 1]
            t_om.append(p.op("vector", lambda h, c=c, j=j, col=col: h.tensor_scalar(
                out=omx[:, c * 3 + j:c * 3 + j + 1], in0=col, scalar1=-1.0, scalar2=1.0,
                op0=ALU.mult, op1=ALU.add), after=[tok_v[c]], inc="dve"))

    bcst = {"i": 0}

    def bcast_piece(src_row_dram, dst_sb, W, after=()):
        i = bcst["i"]
        bcst["i"] += 1
        ps = psC if i % 2 == 0 else psD
        trow = p.dma("sync", rowb[0:1, 0:W], src_row_dram,
                     after=[bcst.get("row")] + list(after))
        tm_ = p.op("tensor", lambda h, W=W, ps=ps: h.matmul(
            ps[0:128, 0:W], ones1f[:], rowb[0:1, 0:W], start=True, stop=True),
            after=[trow, tz, bcst.get(("ps", ps is psD))], inc="pe")
        tev = p.op("vector", lambda h, W=W, ps=ps, dst_sb=dst_sb: h.tensor_copy(
            dst_sb, ps[0:128, 0:W]), after=[tm_], inc="dve")
        bcst["row"] = tev
        bcst[("ps", ps is psD)] = tev
        return tev

    tgb = []
    for r in range(4):
        for b0 in range(0, D, 512):
            tgb.append(bcast_piece(rows_in[r:r + 1, b0:b0 + 512],
                                   gbt[:, r * D + b0:r * D + b0 + 512], 512))
    thv0 = p.dma("sync", rowb[0:1, 0:1], hv_in[:], after=[tgb[-1]])
    thv1 = p.op("tensor", lambda h: h.matmul(psC[0:128, 0:1], ones1f[:], rowb[0:1, 0:1],
                                             start=True, stop=True),
                after=[thv0, tz, tgb[-1]], inc="pe")
    thv = p.op("vector", lambda h: h.tensor_copy(hvb[:], psC[0:128, 0:1]),
               after=[thv1], inc="dve")

    fm_tok = {}
    g5 = {}
    for i in range(ntile):
        sl = i % 2
        xti = xt[:, sl * D:(sl + 1) * D]
        xni = xnt[:, sl * D:(sl + 1) * D]
        xpi = xpt[:, sl * D:(sl + 1) * D]
        hhi = hht[:, sl * D:(sl + 1) * D]
        hbi = hbt[:, sl * D:(sl + 1) * D]
        tld = p.dma("sync", xti, x1_in[i * 128:(i + 1) * 128, :], after=[g5.get(("xt", sl))])
        tln1 = tm_ln_emit(p, stats, 3 * i, xti, xni, D, after=[tld, g5.get(("xn", sl))],
                          kind="ln")
        tln2 = tm_ln_emit(p, stats, 3 * i + 1, xni, xpi, D,
                          after=[tln1, g5.get(("xp", sl)), tgb[-1]], kind="ln",
                          gb=(gbt[:, 0:D], gbt[:, D:2 * D], tgb[-1]))
        tln3 = tm_ln_emit(p, stats, 3 * i + 2, xpi, hhi, D,
                          after=[tln2, g5.get(("hh", sl))], kind="ln",
                          gb=(gbt[:, 2 * D:3 * D], gbt[:, 3 * D:4 * D], tgb[-1]))
        tcb = p.op("scalar", lambda h, hhi=hhi, hbi=hbi: h.copy(out=hbi, in_=hhi),
                   after=[tln3, g5.get(("hb", sl))], inc="act")
        if i >= 1:
            sx1 = p.dma("scalar", xn1_out[(i - 1) * 128:i * 128, :], xni, after=[tln1])
            sx2 = p.dma("scalar", xp_out[(i - 1) * 128:i * 128, :], xpi, after=[tln2])
        else:
            sx1 = sx2 = None
        trs = []
        for c in range(NCH):
            trs.append(tr.transpose(hbi[:, c * 128:(c + 1) * 128],
                                    hfm[:, c * NtE + i * 128: c * NtE + (i + 1) * 128],
                                    after=[tcb]))
        g5[("xt", sl)] = tln1
        g5[("xn", sl)] = sx1 or tln2
        g5[("xp", sl)] = sx2 or tln3
        g5[("hh", sl)] = tcb
        g5[("hb", sl)] = trs[-1]
        fm_tok[i] = trs
    all_fm = [t for i in fm_tok for t in fm_tok[i]]

    thf = []
    for c in range(NCH):
        thf.append(p.op("vector", lambda h, c=c: h.tensor_scalar_mul(
            hfm[:, c * NtE + 127: c * NtE + 128], hfm[:, c * NtE + 127: c * NtE + 128],
            hvb[:, 0:1]), after=all_fm + [thv], inc="dve"))

    stores = []
    prev5 = {}
    for mi, (nm, wb, wtok, outt, sig) in enumerate((
            ("tmk", wkb, t_wk, k_out, False),
            ("tmv", wvb, t_wv, v_out, False),
            ("tmr", wrb, t_wr, sr_out, True))):
        mtoks = []
        for c in range(NCH):
            cur = hfm[:, c * NtE + 128: c * NtE + NtE]
            shf = hfm[:, c * NtE + 127: c * NtE + NtE - 1]
            col = vecs[:, c * NV + vc[nm]: c * NV + vc[nm] + 1]
            oc = omx[:, c * 3 + mi:c * 3 + mi + 1]
            t1 = p.op("vector", lambda h, c=c, cur=cur, col=col: h.tensor_scalar_mul(
                mixb[:, c * Nt:(c + 1) * Nt], cur, col),
                after=thf + [prev5.get(("mix", c)), t_om[-1]], inc="dve")
            t2 = p.op("vector", lambda h, c=c, shf=shf, oc=oc: h.scalar_tensor_tensor(
                out=mixb[:, c * Nt:(c + 1) * Nt], in0=shf, scalar=oc,
                in1=mixb[:, c * Nt:(c + 1) * Nt], op0=ALU.mult, op1=ALU.add),
                after=[t1], inc="dve")
            mtoks.append(t2)
        for f in range(NCH):
            for blk in range(Nt // 512):
                ps = psA if (f * (Nt // 512) + blk) % 2 == 0 else psB
                psn = (f * (Nt // 512) + blk) % 2
                for c in range(NCH):
                    aft = []
                    if c == 0:
                        aft = mtoks + wtok + [prev5.get(("ps", psn))]
                    p.op("tensor", lambda h, f=f, c=c, blk=blk, ps=ps, wb=wb: h.matmul(
                        ps[:], wb[:, c * D + f * 128: c * D + (f + 1) * 128],
                        mixb[:, c * Nt + blk * 512: c * Nt + (blk + 1) * 512],
                        start=(c == 0), stop=(c == NCH - 1)),
                        after=aft, inc="pe" if c == NCH - 1 else None)
                tmm = ("pe", p.cnt["pe"])
                ssl = (f * (Nt // 512) + blk) % 4
                sa = stg[:, ssl * 512:(ssl + 1) * 512]
                if sig:
                    tev = p.op("scalar", lambda h, ps=ps, sa=sa: h.activation(
                        out=sa, in_=ps[:], func=AF.Sigmoid),
                        after=[tmm, prev5.get(("stg", ssl))], inc="act")
                else:
                    tev = p.op("scalar", lambda h, ps=ps, sa=sa: h.copy(out=sa, in_=ps[:]),
                               after=[tmm, prev5.get(("stg", ssl))], inc="act")
                prev5[("ps", psn)] = tev
                tst = p.dma("sync", outt[f * 128:(f + 1) * 128, blk * 512:(blk + 1) * 512],
                            sa, after=[tev])
                prev5[("stg", ssl)] = tst
                stores.append(tst)
        for c in range(NCH):
            prev5[("mix", c)] = ("pe", p.cnt["pe"])
    p.op("gpsimd", None, after=stores)
    p.emit_and_close()
    return nc



def build_p6(CS, NV, vc, LC=2048):
    nc = bass.Bass("TRN2", target_bir_lowering=False,
                   detect_race_conditions=Prog.STRICT)
    k_in = nc.dram_tensor("k1_fm", [CS, T], F32, kind="ExternalInput").ap()
    v_in = nc.dram_tensor("v1_fm", [CS, T], F32, kind="ExternalInput").ap()
    vecs_in = nc.dram_tensor("vecs", [CS, NV], F32, kind="ExternalInput").ap()
    y_out = nc.dram_tensor("y_att_fm", [CS, T], F32, kind="ExternalOutput").ap()

    p = Prog(nc)
    nsub = CS // 128
    nch = T // LC
    kb = p.sb("kb", [128, T], F32)
    vb = p.sb("vb", [128, T], F32)
    sbs = p.sb("sbs", [128, LC + 1], F32)
    sbz = p.sb("sbz", [128, LC + 1], F32)
    t1b = p.sb("t1b", [128, LC], F32)
    numb = p.sb("numb", [128, LC], F32)
    denb = p.sb("denb", [128, LC], F32)
    TAU = p.sb("TAU", [128, LC], F32)
    ystg = p.sb("ystg", [128, 2 * LC], F32)
    vecs = p.sb("vecs_sb", [128, NV], F32)
    sc = p.sb("sc", [128, 16], F32)

    t_tau = p.op("gpsimd", lambda h: h.iota(TAU[:], pattern=[[1, LC]], base=0,
                                            channel_multiplier=0,
                                            allow_small_or_imprecise_dtypes=True),
                 inc="gp")
    stores = []
    g6 = {}
    for sub in range(nsub):
        tv = p.dma("gpsimd", vecs[:], vecs_in[sub * 128:(sub + 1) * 128, :],
                   after=[g6.get("vecs")])
        tkl = p.dma("sync", kb[:], k_in[sub * 128:(sub + 1) * 128, :],
                    after=[g6.get("kb")])
        tvl = p.dma("sync", vb[:], v_in[sub * 128:(sub + 1) * 128, :],
                    after=[g6.get("vb")])
        td = vecs[:, vc["time_decay"]:vc["time_decay"] + 1]
        tf = vecs[:, vc["time_first"]:vc["time_first"] + 1]
        t1 = p.op("scalar", lambda h, td=td: h.activation(out=sc[:, 0:1], in_=td,
                                                          func=AF.Exp),
                  after=[tv, g6.get("sc")], inc="act")
        t2 = p.op("vector", lambda h: h.tensor_scalar_mul(sc[:, 0:1], sc[:, 0:1], -1.0),
                  after=[t1], inc="dve")
        t3 = p.op("scalar", lambda h: h.activation(out=sc[:, 1:2], in_=sc[:, 0:1],
                                                   func=AF.Exp), after=[t2], inc="act")
        t4 = p.op("scalar", lambda h, tf=tf: h.activation(out=sc[:, 2:3], in_=tf,
                                                          func=AF.Exp),
                  after=[tv], inc="act")
        t5 = p.op("gpsimd", lambda h: h.memset(sc[:, 3:4], -1e30), after=[g6.get("sc")],
                  inc="gp")
        t6 = p.op("gpsimd", lambda h: h.memset(sc[:, 4:6], 0.0), inc="gp")
        tconsts = [t2, t3, t4, t5, t6]
        prevy = g6.get("y")
        for ch in range(nch):
            cl = slice(ch * LC, (ch + 1) * LC)
            tM = p.op("vector", lambda h, cl=cl: h.reduce_max(sc[:, 6:7], kb[:, cl],
                                                              axis=AX.X),
                      after=[tkl, g6.get("Mrd")], inc="dve")
            tnM = p.op("vector", lambda h: h.tensor_scalar_mul(sc[:, 7:8], sc[:, 6:7],
                                                               -1.0), after=[tM], inc="dve")
            tE = p.op("scalar", lambda h, cl=cl: h.activation(
                out=kb[:, cl], in_=kb[:, cl], func=AF.Exp, bias=sc[:, 7:8]),
                after=[tnM, g6.get(("eread", ch))], inc="act")
            tEV = p.op("vector", lambda h, cl=cl: h.tensor_tensor(
                vb[:, cl], vb[:, cl], kb[:, cl], ALU.mult),
                after=[tE, tvl, g6.get(("evread", ch))], inc="dve")
            ts_ = p.op("vector", lambda h, cl=cl: h.tensor_tensor_scan(
                sbs[:, 1:LC + 1], sc[:, 1:2].broadcast_to([128, LC]), vb[:, cl], 0.0,
                ALU.mult, ALU.add), after=[tEV, t3, g6.get("sprev")], inc="dve")
            tz_ = p.op("vector", lambda h, cl=cl: h.tensor_tensor_scan(
                sbz[:, 1:LC + 1], sc[:, 1:2].broadcast_to([128, LC]), kb[:, cl], 0.0,
                ALU.mult, ALU.add), after=[tE, g6.get("zprev")], inc="dve")
            tz0 = p.op("gpsimd", lambda h: h.memset(sbs[:, 0:1], 0.0),
                       after=[g6.get("sprev")], inc="gp")
            tz1 = p.op("gpsimd", lambda h: h.memset(sbz[:, 0:1], 0.0),
                       after=[g6.get("zprev")], inc="gp")
            tmM = p.op("vector", lambda h: h.tensor_tensor(sc[:, 8:9], sc[:, 3:4],
                                                           sc[:, 6:7], ALU.subtract),
                       after=[tM, tconsts[3], g6.get("carry")], inc="dve")
            tt1 = p.op("vector", lambda h: h.tensor_scalar(
                out=t1b[:], in0=TAU[:], scalar1=sc[:, 0:1], scalar2=sc[:, 8:9],
                op0=ALU.mult, op1=ALU.add), after=[tmM, t2, t_tau, g6.get("t1")],
                inc="dve")
            te12 = p.op("scalar", lambda h: h.activation(out=t1b[:], in_=t1b[:],
                                                         func=AF.Exp),
                        after=[tt1], inc="act")
            tn1 = p.op("vector", lambda h: h.tensor_scalar_mul(numb[:], t1b[:],
                                                               sc[:, 4:5]),
                       after=[te12, tconsts[4], g6.get("num"), g6.get("carry")],
                       inc="dve")
            tn2 = p.op("vector", lambda h: h.tensor_tensor(numb[:], numb[:],
                                                           sbs[:, 0:LC], ALU.add),
                       after=[tn1, ts_, tz0], inc="dve")
            tn3 = p.op("vector", lambda h, cl=cl: h.scalar_tensor_tensor(
                out=numb[:], in0=vb[:, cl], scalar=sc[:, 2:3], in1=numb[:],
                op0=ALU.mult, op1=ALU.add), after=[tn2, t4], inc="dve")
            td1 = p.op("gpsimd", lambda h: h.tensor_tensor(
                denb[:], t1b[:], sc[:, 5:6].broadcast_to([128, LC]), ALU.mult),
                after=[te12, tconsts[4], g6.get("den"), g6.get("carry")], inc="gp")
            td2 = p.op("gpsimd", lambda h: h.tensor_tensor(denb[:], denb[:],
                                                           sbz[:, 0:LC], ALU.add),
                       after=[td1, tz_, tz1], inc="gp")
            td3 = p.op("gpsimd", lambda h, cl=cl: h.tensor_tensor(
                sbz[:, 0:LC] if False else denb[:], denb[:], None, ALU.add)
                if False else h.tensor_tensor(
                t1b[:], kb[:, cl], sc[:, 2:3].broadcast_to([128, LC]), ALU.mult),
                after=[td2, t4, te12, tn1], inc="gp")
            td4 = p.op("gpsimd", lambda h: h.tensor_tensor(denb[:], denb[:], t1b[:],
                                                           ALU.add),
                       after=[td3], inc="gp")
            tr1 = p.op("vector", lambda h: h.reciprocal(denb[:], denb[:]),
                       after=[td4, tn3], inc="dve")
            ysl = ystg[:, (ch % 2) * LC:(ch % 2) * LC + LC]
            tr2 = p.op("vector", lambda h, ysl=ysl: h.tensor_tensor(
                ysl, numb[:], denb[:], ALU.mult),
                after=[tr1, g6.get(("yst", ch % 2))], inc="dve")
            tst = p.dma("sync", y_out[sub * 128:(sub + 1) * 128, cl], ysl, after=[tr2])
            stores.append(tst)
            g6[("yst", ch % 2)] = tst
            tc1 = p.op("vector", lambda h: h.scalar_tensor_tensor(
                out=sc[:, 9:10], in0=sc[:, 0:1], scalar=float(LC), in1=sc[:, 3:4],
                op0=ALU.mult, op1=ALU.add), after=[tn3, td4, tmM], inc="dve")
            tc2 = p.op("vector", lambda h: h.tensor_tensor(sc[:, 10:11], sc[:, 9:10],
                                                           sc[:, 6:7], ALU.max),
                       after=[tc1], inc="dve")
            tc3 = p.op("vector", lambda h: h.tensor_tensor(sc[:, 9:10], sc[:, 9:10],
                                                           sc[:, 10:11], ALU.subtract),
                       after=[tc2], inc="dve")
            tc4 = p.op("scalar", lambda h: h.activation(out=sc[:, 11:12], in_=sc[:, 9:10],
                                                        func=AF.Exp), after=[tc3], inc="act")
            tc5 = p.op("vector", lambda h: h.tensor_tensor(sc[:, 13:14], sc[:, 6:7],
                                                           sc[:, 10:11], ALU.subtract),
                       after=[tc2], inc="dve")
            tc6 = p.op("scalar", lambda h: h.activation(out=sc[:, 12:13], in_=sc[:, 13:14],
                                                        func=AF.Exp), after=[tc5], inc="act")
            tc7 = p.op("vector", lambda h: h.tensor_tensor(sc[:, 4:5], sc[:, 4:5],
                                                           sc[:, 11:12], ALU.mult),
                       after=[tc4, tn1], inc="dve")
            tc8 = p.op("vector", lambda h: h.scalar_tensor_tensor(
                out=sc[:, 4:5], in0=sbs[:, LC:LC + 1], scalar=sc[:, 12:13],
                in1=sc[:, 4:5], op0=ALU.mult, op1=ALU.add),
                after=[tc7, tc6, ts_], inc="dve")
            tc9 = p.op("vector", lambda h: h.tensor_tensor(sc[:, 5:6], sc[:, 5:6],
                                                           sc[:, 11:12], ALU.mult),
                       after=[tc4, td1], inc="dve")
            tc10 = p.op("vector", lambda h: h.scalar_tensor_tensor(
                out=sc[:, 5:6], in0=sbz[:, LC:LC + 1], scalar=sc[:, 12:13],
                in1=sc[:, 5:6], op0=ALU.mult, op1=ALU.add),
                after=[tc9, tc6, tz_], inc="dve")
            tc11 = p.op("vector", lambda h: h.tensor_copy(sc[:, 3:4], sc[:, 10:11]),
                        after=[tc10, tc8, tmM, tc3 if False else tc2], inc="dve")
            g6["carry"] = tc11
            g6["Mrd"] = tc11
            g6[("eread", ch + 1)] = td3
            g6[("evread", ch + 1)] = tn3
            g6["sprev"] = tc8
            g6["zprev"] = tc10
            g6["t1"] = td4
            g6["num"] = tr2
            g6["den"] = tr2
        g6["vecs"] = tc11
        g6["kb"] = td3
        g6["vb"] = tn3
        g6["sc"] = tc11
        g6["y"] = stores[-1]
    p.op("gpsimd", None, after=stores)
    p.emit_and_close()
    return nc



def build_p7(Nt, NV, vc):
    nc = bass.Bass("TRN2", target_bir_lowering=False,
                   detect_race_conditions=Prog.STRICT)
    NtE = Nt + 128
    ntile = NtE // 128
    notile = Nt // 128
    ya_in = nc.dram_tensor("ya_ext", [D, NtE], F32, kind="ExternalInput").ap()
    sr_in = nc.dram_tensor("sr_ext", [D, NtE], F32, kind="ExternalInput").ap()
    xp_in = nc.dram_tensor("xp_ext", [NtE, D], F32, kind="ExternalInput").ap()
    xn1_in = nc.dram_tensor("xn1_rows", [Nt, D], F32, kind="ExternalInput").ap()
    x1_in = nc.dram_tensor("x1_rows", [Nt, D], F32, kind="ExternalInput").ap()
    hv_in = nc.dram_tensor("halo_valid", [1, 1], F32, kind="ExternalInput").ap()
    wo_in = nc.dram_tensor("WoT", [D, D], F32, kind="ExternalInput").ap()
    wfk_in = nc.dram_tensor("WfkT", [D, 4096], F32, kind="ExternalInput").ap()
    wfr_in = nc.dram_tensor("WfrT", [D, D], F32, kind="ExternalInput").ap()
    wfv_in = nc.dram_tensor("WfvT", [4096, D], F32, kind="ExternalInput").ap()
    vecs_in = nc.dram_tensor("vecs", [D, NV], F32, kind="ExternalInput").ap()
    rows_in = nc.dram_tensor("p7rows", [2, D], F32, kind="ExternalInput").ap()
    fgw_in = nc.dram_tensor("f_gateT", [D, 8], F32, kind="ExternalInput").ap()
    fw1_in = nc.dram_tensor("f_w1", [8, D, D], F32, kind="ExternalInput").ap()
    fw2_in = nc.dram_tensor("f_w2", [8, D, D], F32, kind="ExternalInput").ap()
    fw3_in = nc.dram_tensor("f_w3", [8, D, D], F32, kind="ExternalInput").ap()
    fw1s_in = nc.dram_tensor("f_w1s", [D, D], F32, kind="ExternalInput").ap()
    fw2s_in = nc.dram_tensor("f_w2s", [D, D], F32, kind="ExternalInput").ap()
    fw3s_in = nc.dram_tensor("f_w3s", [D, D], F32, kind="ExternalInput").ap()
    out_out = nc.dram_tensor("out_rows", [Nt, D], F32, kind="ExternalOutput").ap()
    _k7 = "ExternalOutput" if os.environ.get("P7DBG") == "1" else "Internal"
    xpp_scr = nc.dram_tensor("xpp_scr", [NtE, D], F32, kind=_k7).ap()
    s1_scr = nc.dram_tensor("s1_scr", [Nt, D], F32, kind=_k7).ap()
    kv_scr = nc.dram_tensor("kv_scr", [Nt, D], F32, kind=_k7).ap()
    wfk_bscr = nc.dram_tensor("wfk_bscr", [D, 4096], BF16, kind="Internal").ap()
    wfv_bscr = nc.dram_tensor("wfv_bscr", [4096, D], BF16, kind="Internal").ap()
    dbg_wm = dbg_moe = None
    if os.environ.get("P7DBG") == "1":
        dbg_wm = nc.dram_tensor("dbg_wmask", [128, 128], F32,
                                kind="ExternalOutput").ap()
        dbg_moe = nc.dram_tensor("dbg_moe", [Nt, D], F32,
                                 kind="ExternalOutput").ap()
        dbg_hh = nc.dram_tensor("dbg_hh", [128, 8192], BF16,
                                kind="ExternalOutput").ap()
        dbg_xn = nc.dram_tensor("dbg_xn", [128, 16384], BF16,
                                kind="ExternalOutput").ap()
        dbg_ew = nc.dram_tensor("dbg_ew", [128, 8192], BF16,
                                kind="ExternalOutput").ap()

    p = Prog(nc)

    mixk = p.sb("mixk", [128, NCH * Nt], BF16)
    xn1fm = mixk
    accb = p.sb("accb", [128, 8 * D], F32)
    vecs = p.sb("vecs_sb", [128, NCH * NV], F32)
    stats = p.sb("stats", [128, 32 * 2 * ntile], F32)
    gbt = p.sb("gbt", [128, 2 * D], F32)
    wmask = p.sb("wmask", [128, 8 * 16], F32)
    xt1 = p.sb("xt1", [128, 2 * D], F32)
    xt2 = p.sb("xt2", [128, 2 * D], F32)
    xb1 = p.sb("xb1", [128, 2 * D], BF16)
    lob = p.sb("lob", [128, D], BF16)
    rowb = p.sb("rowb", [1, 512], F32)
    wrow = p.sb("wrow", [1, 2 * D], F32)
    iotaE = p.sb("iotaE", [128, 8], F32)
    rnkb = p.sb("rnkb", [128, 80], F32)
    scr = p.sb("scr", [128, 32], F32)
    ones1f = p.sb("ones1f", [1, 128], F32)
    hvb = p.sb("hvb", [128, 1], F32)
    omx = p.sb("omx", [128, NCH * 2], F32)
    ident = p.sb("ident", [128, 128], BF16)
    iota2 = p.sb("iota2", [128, 128], F32)
    arena = Arena(p, "arena", 43008)
    AT = arena.t
    afm = AT[:, 0:NCH * NtE]
    wob = AT[:, NCH * NtE:NCH * NtE + NCH * D]
    h2fm = AT[:, 25600:43008]
    wfkst = AT[:, 0:4096]
    wfvst = AT[:, 4096:6144]
    kfs = AT[:, 6144:6400]
    kfsq = AT[:, 6400:6656]
    kvstg = AT[:, 6656:10752].bitcast(F32)
    h1fm = AT[:, 0:NCH * Nt // 2]
    inb = AT[:, 8192:8192 + NCH * Nt // 2]
    ew1 = AT[:, 16384:16384 + NCH * D]
    ew3 = AT[:, 24576:24576 + NCH * D]
    ew2 = AT[:, 32768:32768 + NCH * D]
    web = AT[:, 40960:40960 + 2 * D].bitcast(F32)

    psA = p.psum("psA", [128, 512], F32)
    psB = p.psum("psB", [128, 512], F32)
    psC = p.psum("psC", [128, 512], F32)
    psD = p.psum("psD", [128, 512], F32)
    ps0 = p.psum("ps0", [128, 512], F32)
    ps1 = p.psum("ps1", [128, 512], F32)
    tid = emit_identity(p, ident, iota2)
    tr = Transposer(p, tid, ident)

    tok_v = [p.dma("gpsimd", vecs[:, c * NV:(c + 1) * NV], vecs_in[c * 128:(c + 1) * 128, :])
             for c in range(NCH)]
    t_wo = [p.dma("gpsimd", wob[:, c * D:(c + 1) * D], wo_in[c * 128:(c + 1) * 128, :])
            for c in range(NCH)]
    tz = p.op("gpsimd", lambda h: h.memset(ones1f[:], 1.0), inc="gp")
    t_iE = p.op("gpsimd", lambda h: h.iota(iotaE[:], pattern=[[1, 8]], base=0,
                                           channel_multiplier=0,
                                           allow_small_or_imprecise_dtypes=True), inc="gp")
    t_om = []
    for c in range(NCH):
        for j, nm in enumerate(("cmk", "cmr")):
            col = vecs[:, c * NV + vc[nm]: c * NV + vc[nm] + 1]
            t_om.append(p.op("vector", lambda h, c=c, j=j, col=col: h.tensor_scalar(
                out=omx[:, c * 2 + j:c * 2 + j + 1], in0=col, scalar1=-1.0, scalar2=1.0,
                op0=ALU.mult, op1=ALU.add), after=[tok_v[c]], inc="dve"))

    bcst = {"i": 0}

    def bcast_piece(src_row_dram, dst_sb, W, after=()):
        i = bcst["i"]
        bcst["i"] += 1
        ps = psC if i % 2 == 0 else psD
        trow = p.dma("sync", rowb[0:1, 0:W], src_row_dram,
                     after=[bcst.get("row")] + list(after))
        tm_ = p.op("tensor", lambda h, W=W, ps=ps: h.matmul(
            ps[0:128, 0:W], ones1f[:], rowb[0:1, 0:W], start=True, stop=True),
            after=[trow, tz, bcst.get(("ps", ps is psD))], inc="pe")
        tev = p.op("vector", lambda h, W=W, ps=ps, dst_sb=dst_sb: h.tensor_copy(
            dst_sb, ps[0:128, 0:W]), after=[tm_], inc="dve")
        bcst["row"] = tev
        bcst[("ps", ps is psD)] = tev
        return tev

    tgb = [bcast_piece(rows_in[r:r + 1, b0:b0 + 512], gbt[:, r * D + b0:r * D + b0 + 512],
                       512) for r in range(2) for b0 in range(0, D, 512)]
    thv0 = p.dma("sync", rowb[0:1, 0:1], hv_in[:], after=[tgb[-1]])
    thv1 = p.op("tensor", lambda h: h.matmul(psC[0:128, 0:1], ones1f[:], rowb[0:1, 0:1],
                                             start=True, stop=True),
                after=[thv0, tz, tgb[-1]], inc="pe")
    thv = p.op("vector", lambda h: h.tensor_copy(hvb[:], psC[0:128, 0:1]),
               after=[thv1], inc="dve")

    pre_t = []
    prep_prev = None
    for src_, dst_, rows in ((wfk_in, wfk_bscr, D), (wfv_in, wfv_bscr, 4096)):
        for r0 in range(0, rows, 128):
            tld = p.dma("gpsimd", mixk[:, 0:4096] if rows == D else mixk[:, 0:D],
                        src_[r0:r0 + 128, :], after=[prep_prev])
            tst = p.dma("scalar", dst_[r0:r0 + 128, :],
                        mixk[:, 0:4096] if rows == D else mixk[:, 0:D], after=[tld])
            prep_prev = tst
            pre_t.append(tst)

    z1 = {}
    a_done = []
    for c in range(NCH):
        sl = c % 2
        stg = xn1fm.bitcast(F32)
        tsr = p.dma("sync", stg[:, 0:NtE], sr_in[c * 128:(c + 1) * 128, :],
                    after=[z1.get("sr"), pre_t[-1] if c == 0 else None])
        tya = p.dma("scalar", stg[:, NtE:2 * NtE], ya_in[c * 128:(c + 1) * 128, :],
                    after=[z1.get("ya"), pre_t[-1] if c == 0 else None])
        tmu = p.op("vector", lambda h, c=c, stg=stg: h.tensor_tensor(
            afm[:, c * NtE:(c + 1) * NtE], stg[:, 0:NtE], stg[:, NtE:2 * NtE], ALU.mult),
            after=[tsr, tya, pre_t[-1] if c == 0 else None], inc="dve")
        z1["sr"] = tmu
        z1["ya"] = tmu
        a_done.append(tmu)

    z2 = {}
    h2_toks = {}
    for i in range(ntile):
        sl = i % 2
        xpt_ = xt1[:, sl * D:(sl + 1) * D]
        xppt = xt2[:, sl * D:(sl + 1) * D]
        hbt_ = xb1[:, sl * D:(sl + 1) * D]
        tld = p.dma("sync", xpt_, xp_in[i * 128:(i + 1) * 128, :],
                    after=[z2.get(("xp", sl))])
        for half in range(2):
            ps = psA if half == 0 else psB
            for c in range(NCH):
                aft = []
                if c == 0:
                    aft = a_done + t_wo + [z2.get(("ps", half))]
                p.op("tensor", lambda h, c=c, i=i, half=half, ps=ps: h.matmul(
                    ps[:], afm[:, c * NtE + i * 128: c * NtE + (i + 1) * 128],
                    wob[:, c * D + half * 512: c * D + half * 512 + 512],
                    start=(c == 0), stop=(c == NCH - 1)),
                    after=aft, inc="pe" if c == NCH - 1 else None)
            tmm = ("pe", p.cnt["pe"])
            tadd = p.op("vector", lambda h, ps=ps, half=half, xppt=xppt, xpt_=xpt_: h.tensor_tensor(
                xppt[:, half * 512:(half + 1) * 512], ps[:],
                xpt_[:, half * 512:(half + 1) * 512], ALU.add),
                after=[tmm, tld, z2.get(("xpp", sl))], inc="dve")
            z2[("ps", half)] = tadd
        tst = p.dma("scalar", xpp_scr[i * 128:(i + 1) * 128, :], xppt, after=[tadd])
        tln = tm_ln_emit(p, stats, i, xppt, xppt, D, after=[tadd, tst], kind="ln",
                         gb=(gbt[:, 0:D], gbt[:, D:2 * D], tgb[-1]))
        tcb = p.op("scalar", lambda h, xppt=xppt, hbt_=hbt_: h.copy(out=hbt_, in_=xppt),
                   after=[tln, z2.get(("hb", sl))], inc="act")
        trs = []
        for c in range(NCH):
            trs.append(tr.transpose(hbt_[:, c * 128:(c + 1) * 128],
                                    h2fm[:, c * NtE + i * 128: c * NtE + (i + 1) * 128],
                                    after=[tcb]))
        z2[("xp", sl)] = tln
        z2[("xpp", sl)] = tcb
        z2[("hb", sl)] = trs[-1]
        h2_toks[i] = trs
    all_h2 = [t for i in h2_toks for t in h2_toks[i]]
    thf = []
    for c in range(NCH):
        thf.append(p.op("vector", lambda h, c=c: h.tensor_scalar_mul(
            h2fm[:, c * NtE + 127: c * NtE + 128], h2fm[:, c * NtE + 127: c * NtE + 128],
            hvb[:, 0:1]), after=all_h2 + [thv], inc="dve"))

    mix_toks = []
    for c in range(NCH):
        cur = h2fm[:, c * NtE + 128: c * NtE + NtE]
        shf = h2fm[:, c * NtE + 127: c * NtE + NtE - 1]
        col = vecs[:, c * NV + vc["cmk"]: c * NV + vc["cmk"] + 1]
        oc = omx[:, c * 2 + 0:c * 2 + 1]
        t1 = p.op("vector", lambda h, c=c, cur=cur, col=col: h.tensor_scalar_mul(
            mixk[:, c * Nt:(c + 1) * Nt], cur, col),
            after=thf + [t_om[-1], pre_t[-1]], inc="dve")
        t2 = p.op("vector", lambda h, c=c, shf=shf, oc=oc: h.scalar_tensor_tensor(
            out=mixk[:, c * Nt:(c + 1) * Nt], in0=shf, scalar=oc,
            in1=mixk[:, c * Nt:(c + 1) * Nt], op0=ALU.mult, op1=ALU.add),
            after=[t1], inc="dve")
        mix_toks.append(t2)

    z4 = {}
    kv_stores = []
    z2ps = [z2[("ps", 0)], z2[("ps", 1)]]
    for grp in range(Nt // 256):
        g0 = grp * 256
        for f in range(32):
            sl = f % 2
            wst = wfkst[:, sl * 1024:(sl + 1) * 1024]
            aft_wk = [z4.get(("wfk", sl)), pre_t[-1]]
            if aft_wk[0] is None:
                aft_wk += z2ps
            t_wk = p.dma("gpsimd", wst,
                         wfk_bscr.rearrange("(c p) q -> p c q", c=NCH)[:, :, f * 128:(f + 1) * 128],
                         after=aft_wk)
            ps = psA if sl == 0 else psB
            for c in range(NCH):
                aft = mix_toks + a_done
                if c == 0:
                    aft = aft + [t_wk, z4.get(("psk", sl))]
                p.op("tensor", lambda h, c=c, f=f, g0=g0, ps=ps, wst=wst: h.matmul(
                    ps[0:128, 0:256], wst[:, c * 128:(c + 1) * 128],
                    mixk[:, c * Nt + g0: c * Nt + g0 + 256],
                    start=(c == 0), stop=(c == NCH - 1)),
                    after=aft, inc="pe" if c == NCH - 1 else None)
            tmm = ("pe", p.cnt["pe"])
            z4[("wfk", sl)] = tmm
            trl = p.op("scalar", lambda h, ps=ps: h.activation(
                out=kfs[:, 0:256], in_=ps[0:128, 0:256], func=AF.Relu),
                after=[tmm, z4.get("kfs")], inc="act")
            z4[("psk", sl)] = trl
            tsq = p.op("vector", lambda h: h.tensor_tensor(kfsq[:, 0:256], kfs[:, 0:256],
                                                           kfs[:, 0:256], ALU.mult),
                       after=[trl, z4.get("kfsq")], inc="dve")
            z4["kfs"] = tsq
            aft_wv = [z4.get(("wfv", sl))]
            if aft_wv[0] is None:
                aft_wv += z2ps
            t_wv2 = p.dma("gpsimd", wfvst[:, sl * 1024:(sl + 1) * 1024],
                          wfv_bscr[f * 128:(f + 1) * 128, :],
                          after=aft_wv)
            for ti in range(2):
                for half in range(2):
                    psk = (ps0, ps1, psC, psD)[ti * 2 + half]
                    aft = [tsq, t_wv2]
                    if f == 0:
                        aft += [z4.get(("pskv", ti, half)), thv, tgb[-1]]
                    p.op("tensor", lambda h, ti=ti, half=half, psk=psk, sl=sl, f=f: h.matmul(
                        psk[:], kfsq[:, ti * 128:(ti + 1) * 128],
                        wfvst[:, sl * 1024 + half * 512: sl * 1024 + half * 512 + 512],
                        start=(f == 0), stop=(f == 31)),
                        after=aft,
                        inc="pe" if (f == 31 or (ti == 1 and half == 1)) else None)
                    if f == 31:
                        z4[("kvmm", ti, half)] = ("pe", p.cnt["pe"])
            z4[("wfv", sl)] = ("pe", p.cnt["pe"])
            z4["kfsq"] = ("pe", p.cnt["pe"])
        for ti in range(2):
            for half in range(2):
                psk = (ps0, ps1, psC, psD)[ti * 2 + half]
                tev = p.op("vector", lambda h, ti=ti, half=half, psk=psk: h.tensor_copy(
                    kvstg[:, ti * 1024 + half * 512: ti * 1024 + half * 512 + 512],
                    psk[:]), after=[z4[("kvmm", ti, half)], z4.get(("kvstg", ti, half))],
                    inc="dve")
                z4[("pskv", ti, half)] = tev
            tst = p.dma("sync", kv_scr[g0 + ti * 128: g0 + (ti + 1) * 128, :],
                        kvstg[:, ti * 1024:(ti + 1) * 1024], after=[z4[("pskv", ti, 0)],
                                                                    z4[("pskv", ti, 1)]])
            kv_stores.append(tst)
            z4[("kvstg", ti, 0)] = tst
            z4[("kvstg", ti, 1)] = tst

    kvmm_all = [z4[("kvmm", ti, half)] for ti in range(2) for half in range(2)]
    kv_last = kv_stores[-8:]
    wfrb = AT[:, 0:NCH * D]
    t_wfr = [p.dma("gpsimd", wfrb[:, c * D:(c + 1) * D], wfr_in[c * 128:(c + 1) * 128, :],
                   after=(kvmm_all + kv_stores +
                          [z4.get(("wfk", 0)), z4.get(("wfk", 1)),
                           z4.get(("wfv", 0)), z4.get(("wfv", 1)),
                           z4.get("kfsq"), z4.get("kfs")]) if c == 0 else ())
             for c in range(NCH)]
    mixr_toks = []
    for c in range(NCH):
        cur = h2fm[:, c * NtE + 128: c * NtE + NtE]
        shf = h2fm[:, c * NtE + 127: c * NtE + NtE - 1]
        col = vecs[:, c * NV + vc["cmr"]: c * NV + vc["cmr"] + 1]
        oc = omx[:, c * 2 + 1:c * 2 + 2]
        t1 = p.op("vector", lambda h, c=c, cur=cur, col=col: h.tensor_scalar_mul(
            mixk[:, c * Nt:(c + 1) * Nt], cur, col),
            after=kvmm_all + [z4.get(("wfk", 0)), z4.get(("wfk", 1))], inc="dve")
        t2 = p.op("vector", lambda h, c=c, shf=shf, oc=oc: h.scalar_tensor_tensor(
            out=mixk[:, c * Nt:(c + 1) * Nt], in0=shf, scalar=oc,
            in1=mixk[:, c * Nt:(c + 1) * Nt], op0=ALU.mult, op1=ALU.add),
            after=[t1], inc="dve")
        mixr_toks.append(t2)
    z5 = {}
    s1_stores = []
    for i in range(notile):
        sl = i % 2
        srkt = xt1[:, sl * D:(sl + 1) * D]
        kvt = xt2[:, sl * D:(sl + 1) * D]
        for half in range(2):
            ps = psA if half == 0 else psB
            for c in range(NCH):
                aft = mixr_toks
                if c == 0:
                    aft = aft + [z5.get(("ps", half))]
                if c == 0:
                    aft = aft + t_wfr
                p.op("tensor", lambda h, c=c, i=i, half=half, ps=ps: h.matmul(
                    ps[:], mixk[:, c * Nt + i * 128: c * Nt + (i + 1) * 128],
                    wfrb[:, c * D + half * 512: c * D + half * 512 + 512],
                    start=(c == 0), stop=(c == NCH - 1)),
                    after=aft, inc="pe" if c == NCH - 1 else None)
            tmm = ("pe", p.cnt["pe"])
            tsg = p.op("scalar", lambda h, ps=ps, srkt=srkt, half=half: h.activation(
                out=srkt[:, half * 512:(half + 1) * 512], in_=ps[:], func=AF.Sigmoid),
                after=[tmm, z5.get(("srk", sl))], inc="act")
            z5[("ps", half)] = tsg
        tkv = p.dma("sync", kvt, kv_scr[i * 128:(i + 1) * 128, :],
                    after=[kv_stores[i // 2] if False else None,
                           z5.get(("kv", sl))] + kv_stores)
        tm1 = p.op("vector", lambda h, srkt=srkt, kvt=kvt: h.tensor_tensor(
            kvt, kvt, srkt, ALU.mult), after=[tkv, tsg], inc="dve")
        tx = p.dma("scalar", srkt, xpp_scr[(i + 1) * 128:(i + 2) * 128, :],
                   after=[tsg, tm1, z5.get(("xpp", sl))])
        tm2 = p.op("vector", lambda h, srkt=srkt, kvt=kvt: h.tensor_tensor(
            kvt, kvt, srkt, ALU.add), after=[tx, tm1], inc="dve")
        tst = p.dma("sync", s1_scr[i * 128:(i + 1) * 128, :], kvt, after=[tm2])
        s1_stores.append(tst)
        z5[("srk", sl)] = tm2
        z5[("kv", sl)] = tst
        z5[("xpp", sl)] = tm2
    fgwb = p.sb("fgwb", [128, NCH * 16], BF16)
    t_fg = [p.dma("gpsimd", fgwb[:, c * 16:c * 16 + 8], fgw_in[c * 128:(c + 1) * 128, :])
            for c in range(NCH)]
    prevlo = None
    for c in range(NCH):
        tld = p.dma("sync", scr[:, 16:24], fgw_in[c * 128:(c + 1) * 128, :],
                    after=[prevlo, tgb[-1], thv0])
        tsub = p.op("vector", lambda h, c=c: h.scalar_tensor_tensor(
            out=scr[:, 24:32], in0=fgwb[:, c * 16:c * 16 + 8], scalar=-1.0,
            in1=scr[:, 16:24], op0=ALU.mult, op1=ALU.add),
            after=[tld, t_fg[c]], inc="dve")
        prevlo = p.op("scalar", lambda h, c=c: h.copy(
            out=fgwb[:, c * 16 + 8:c * 16 + 16], in_=scr[:, 24:32]),
            after=[tsub], inc="act")
    t_fglo = [prevlo]

    z6 = {}
    router_done = []
    xn_stored = {}
    for i in range(notile):
        sl = i % 2
        xnt_ = xt1[:, sl * D:(sl + 1) * D]
        xbh = xb1[:, 0:D]
        xbl = xb1[:, D:2 * D]
        tld = p.dma("sync", xnt_, xn1_in[i * 128:(i + 1) * 128, :],
                    after=[z6.get(("xn", sl)), z5.get(("srk", sl)), z5.get(("kv", sl))] +
                          s1_stores)
        xn_stored[i] = tld
        tch = p.op("scalar", lambda h, xnt_=xnt_: h.copy(out=xbh, in_=xnt_),
                   after=[tld, z6.get("xbh"), z6.get("xbh2")], inc="act")
        tlo = p.op("vector", lambda h, xnt_=xnt_, i=i: h.scalar_tensor_tensor(
            out=xt2[:, (i % 2) * D:((i % 2) + 1) * D], in0=xbh, scalar=-1.0, in1=xnt_,
            op0=ALU.mult, op1=ALU.add),
            after=[tch, z6.get(("lo", sl))], inc="dve")
        tcl = p.op("scalar", lambda h, i=i: h.copy(
            out=xbl, in_=xt2[:, (i % 2) * D:((i % 2) + 1) * D]),
            after=[tlo, z6.get("xbl")], inc="act")
        trh = []
        trl = []
        for c in range(NCH):
            trh.append(tr.transpose(xbh[:, c * 128:(c + 1) * 128],
                                    xn1fm[:, c * Nt + i * 128: c * Nt + (i + 1) * 128],
                                    after=[tch]))
            trl.append(tr.transpose(xbl[:, c * 128:(c + 1) * 128],
                                    lob[:, c * 128:(c + 1) * 128],
                                    after=[tcl, z6.get(("lob", c))]))
        for j, (xsrc, wcol) in enumerate((("hi", 0), ("lo", 0), ("hi", 8))):
            for c in range(NCH):
                lhs = (xn1fm[:, c * Nt + i * 128: c * Nt + (i + 1) * 128]
                       if xsrc == "hi" else lob[:, c * 128:(c + 1) * 128])
                aft = list(trh) + list(trl)
                if c == 0 and j == 0:
                    aft += t_fglo + t_fg + [z6.get("ps0r")] + kvmm_all
                p.op("tensor", lambda h, lhs=lhs, wcol=wcol, c=c, j=j: h.matmul(
                    ps0[0:128, 0:8], lhs, fgwb[:, c * 16 + wcol: c * 16 + wcol + 8],
                    start=(c == 0 and j == 0), stop=(c == NCH - 1 and j == 2)),
                    after=aft, inc="pe" if (c == NCH - 1 and j == 2) else None)
        tmm = ("pe", p.cnt["pe"])
        for c in range(NCH):
            z6[("lob", c)] = tmm
        lg = scr
        t0 = p.op("vector", lambda h: h.tensor_copy(lg[:, 0:8], ps0[0:128, 0:8]),
                  after=[tmm, z6.get("lg")], inc="dve")
        z6["ps0r"] = t0
        t1s = p.op("vector", lambda h: h.reduce_max(lg[:, 8:9], lg[:, 0:8], axis=AX.X),
                   after=[t0], inc="dve")
        t2s = p.op("vector", lambda h: h.tensor_scalar(out=lg[:, 16:24], in0=lg[:, 0:8],
                                                       scalar1=lg[:, 8:9], scalar2=None,
                                                       op0=ALU.subtract),
                   after=[t1s], inc="dve")
        t3s = p.op("scalar", lambda h: h.activation(out=lg[:, 16:24], in_=lg[:, 16:24],
                                                    func=AF.Exp), after=[t2s], inc="act")
        t4s = p.op("vector", lambda h: h.reduce_sum(lg[:, 9:10], lg[:, 16:24],
                                                    axis=AX.X), after=[t3s], inc="dve")
        t5s = p.op("vector", lambda h: h.reciprocal(lg[:, 10:11], lg[:, 9:10]),
                   after=[t4s], inc="dve")
        t6s = p.op("vector", lambda h: h.tensor_scalar_mul(lg[:, 16:24], lg[:, 16:24],
                                                           lg[:, 10:11]),
                   after=[t5s], inc="dve")
        t7s = p.op("vector", lambda h: h.tensor_tensor(
            rnkb[:, 0:64].rearrange("p (e a) -> p e a", e=8),
            lg[:, 0:8].unsqueeze(1).broadcast_to([128, 8, 8]),
            lg[:, 0:8].unsqueeze(2).broadcast_to([128, 8, 8]),
            ALU.is_gt), after=[t0, z6.get("rnkb")], inc="dve")
        t8s = p.op("vector", lambda h: h.reduce_sum(
            rnkb[:, 64:72], rnkb[:, 0:64].rearrange("p (e a) -> p e a", e=8),
            axis=AX.X), after=[t7s], inc="dve")
        t9s = p.op("vector", lambda h: h.tensor_single_scalar(rnkb[:, 64:72],
                                                              rnkb[:, 64:72], 1.5,
                                                              ALU.is_lt),
                   after=[t8s], inc="dve")
        t10 = p.op("vector", lambda h, i=i: h.tensor_tensor(
            wmask[:, 8 * i:8 * i + 8], lg[:, 16:24], rnkb[:, 64:72], ALU.mult),
            after=[t9s, t6s], inc="dve")
        z6["rnkb"] = t10
        z6["lg"] = t10
        z6[("xn", sl)] = t10
        z6[("lo", sl)] = tcl
        z6["xbh"] = trh[-1]
        z6["xbh2"] = tlo
        z6["xbl"] = trl[-1]
        router_done.append(t10)

    if dbg_wm is not None:
        p.dma("sync", dbg_wm[:, 0:8 * notile], wmask[:, 0:8 * notile],
              after=router_done)
    d7 = {}
    out_stores = []
    nhalf = notile // 2
    for H in range(2):
        tiles = list(range(H * nhalf, (H + 1) * nhalf))
        hbase = H * nhalf * 128
        for ei in range(9):
            shared = ei == 0
            e = ei - 1
            w1s_, w2s_, w3s_ = (fw1s_in, fw2s_in, fw3s_in) if shared else (
                fw1_in[e], fw2_in[e], fw3_in[e])
            t_w1 = [p.dma("gpsimd", ew1[:, c * D:(c + 1) * D], w1s_[c * 128:(c + 1) * 128, :],
                          after=[d7.get("ew1")] +
                                (router_done if (H == 0 and ei == 0) else [])
                                if c == 0 else ())
                    for c in range(NCH)]
            t_w3 = [p.dma("gpsimd", ew3[:, c * D:(c + 1) * D], w3s_[c * 128:(c + 1) * 128, :],
                          after=[d7.get("ew3")] +
                                (router_done if (H == 0 and ei == 0) else [])
                                if c == 0 else ())
                    for c in range(NCH)]
            t_w2 = [p.dma("gpsimd", ew2[:, c * D:(c + 1) * D], w2s_[c * 128:(c + 1) * 128, :],
                          after=[d7.get("ew2")] +
                                (router_done if (H == 0 and ei == 0) else [])
                                if c == 0 else ())
                    for c in range(NCH)]
            if dbg_moe is not None and H == 0 and ei == 1:
                p.dma("gpsimd", dbg_ew[:, 0:NCH * D], ew1[:, 0:NCH * D],
                      after=list(t_w1))
            if shared:
                src_fm = xn1fm
                src_off = hbase
                tin = []
            else:
                twr = []
                for j, i in enumerate(tiles):
                    twr.append(p.dma("sync", wrow[0:1, j * 128:(j + 1) * 128],
                                     wmask[:, 8 * i + e:8 * i + e + 1],
                                     after=[router_done[i], d7.get("wrow")]))
                tb = []
                for b0 in range(0, nhalf * 128, 512):
                    W = min(512, nhalf * 128 - b0)
                    ps = psC if (b0 // 512) % 2 == 0 else psD
                    tm_ = p.op("tensor", lambda h, b0=b0, ps=ps, W=W: h.matmul(
                        ps[0:128, 0:W], ones1f[:], wrow[0:1, b0:b0 + W],
                        start=True, stop=True),
                        after=twr + [tz, d7.get(("psw", (b0 // 512) % 2))], inc="pe")
                    te_ = p.op("vector", lambda h, b0=b0, ps=ps, W=W: h.tensor_copy(
                        web[:, b0:b0 + W], ps[0:128, 0:W]), after=[tm_], inc="dve")
                    d7[("psw", (b0 // 512) % 2)] = te_
                    tb.append(te_)
                d7["wrow"] = tb[-1]
                tin = []
                for c in range(NCH):
                    tin.append(p.op("vector", lambda h, c=c, hbase=hbase: h.tensor_tensor(
                        inb[:, c * (nhalf * 128):(c + 1) * (nhalf * 128)],
                        xn1fm[:, c * Nt + hbase: c * Nt + hbase + nhalf * 128],
                        web[:, 0:nhalf * 128], ALU.mult),
                        after=tb + [d7.get("inb"), d7.get("inbdump")], inc="dve"))
                src_fm = inb
                src_off = 0
            NW = nhalf * 128
            hh_toks = []
            nblk = (NW + 511) // 512
            for f in range(NCH):
                for blk in range(nblk):
                    BW = min(512, NW - blk * 512)
                    ps = psA if (f * nblk + blk) % 2 == 0 else psB
                    psn = (f * nblk + blk) % 2
                    for c in range(NCH):
                        aft = []
                        if c == 0:
                            aft = list(tin) + t_w1 + [d7.get(("psh", psn))]
                        p.op("tensor", lambda h, f=f, c=c, blk=blk, ps=ps, BW=BW,
                             src_fm=src_fm, src_off=src_off, NW=NW: h.matmul(
                            ps[0:128, 0:BW],
                            ew1[:, c * D + f * 128: c * D + (f + 1) * 128],
                            src_fm[:, c * (NW if src_fm is inb else Nt) + src_off +
                                   blk * 512: c * (NW if src_fm is inb else Nt) +
                                   src_off + blk * 512 + BW],
                            start=(c == 0), stop=(c == NCH - 1)),
                            after=aft, inc="pe" if c == NCH - 1 else None)
                    tmm1 = ("pe", p.cnt["pe"])
                    tsi = p.op("scalar", lambda h, f=f, blk=blk, NW=NW, ps=ps,
                               BW=BW: h.activation(
                        out=h1fm[:, f * NW + blk * 512: f * NW + blk * 512 + BW],
                        in_=ps[0:128, 0:BW], func=AF.Silu),
                        after=[tmm1, d7.get(("h1", f, blk))], inc="act")
                    d7[("psh", psn)] = tsi
                    for c in range(NCH):
                        aft = []
                        if c == 0:
                            aft = t_w3 + [tsi]
                        p.op("tensor", lambda h, f=f, c=c, blk=blk, ps=ps, BW=BW,
                             src_fm=src_fm, src_off=src_off, NW=NW: h.matmul(
                            ps[0:128, 0:BW],
                            ew3[:, c * D + f * 128: c * D + (f + 1) * 128],
                            src_fm[:, c * (NW if src_fm is inb else Nt) + src_off +
                                   blk * 512: c * (NW if src_fm is inb else Nt) +
                                   src_off + blk * 512 + BW],
                            start=(c == 0), stop=(c == NCH - 1)),
                            after=aft, inc="pe" if c == NCH - 1 else None)
                    tmm3 = ("pe", p.cnt["pe"])
                    th3 = p.op("vector", lambda h, f=f, blk=blk, ps=ps, NW=NW,
                               BW=BW: h.tensor_tensor(
                        h1fm[:, f * NW + blk * 512: f * NW + blk * 512 + BW],
                        h1fm[:, f * NW + blk * 512: f * NW + blk * 512 + BW],
                        ps[0:128, 0:BW], ALU.mult), after=[tmm3], inc="dve")
                    d7[("psh", psn)] = th3
                    d7[("h1", f, blk)] = th3
                    hh_toks.append(th3)
            if dbg_moe is not None and H == 0 and ei == 1:
                tdh = p.dma("scalar", dbg_hh[:, 0:NW * NCH], h1fm[:, 0:NW * NCH],
                            after=hh_toks)
                d7["inbdump"] = p.dma("scalar", dbg_xn[:, 8192:8192 + NCH * NW],
                                      inb[:, 0:NCH * NW], after=list(tin))
                p.dma("scalar", dbg_xn[:, 0:2 * NW].bitcast(F32), web[:, 0:NW],
                      after=[tb[-1]] + hh_toks)
                for f_ in range(NCH):
                    for blk_ in range(nblk):
                        d7[("h1", f_, blk_)] = tdh
            for j, i in enumerate(tiles):
                for half in range(2):
                    ps = ps0 if half == 0 else ps1
                    for f in range(NCH):
                        aft = []
                        if f == 0:
                            aft = list(hh_toks) + t_w2 + [d7.get(("pso", half))]
                        p.op("tensor", lambda h, f=f, j=j, half=half, ps=ps, NW=NW: h.matmul(
                            ps[:], h1fm[:, f * NW + j * 128: f * NW + (j + 1) * 128],
                            ew2[:, f * D + half * 512: f * D + half * 512 + 512],
                            start=(f == 0), stop=(f == NCH - 1)),
                            after=aft, inc="pe" if f == NCH - 1 else None)
                    tmm = ("pe", p.cnt["pe"])
                    if shared:
                        tac = p.op("vector", lambda h, j=j, half=half, ps=ps: h.tensor_copy(
                            accb[:, j * D + half * 512: j * D + half * 512 + 512], ps[:]),
                            after=[tmm, d7.get(("acc", j, half))], inc="dve")
                    else:
                        tac = p.op("vector", lambda h, j=j, half=half, ps=ps: h.tensor_tensor(
                            accb[:, j * D + half * 512: j * D + half * 512 + 512],
                            accb[:, j * D + half * 512: j * D + half * 512 + 512],
                            ps[:], ALU.add),
                            after=[tmm, d7.get(("acc", j, half))], inc="dve")
                    d7[("pso", half)] = tac
                    d7[("acc", j, half)] = tac
            d7["ew1"] = ("pe", p.cnt["pe"])
            d7["ew3"] = ("pe", p.cnt["pe"])
            d7["ew2"] = ("pe", p.cnt["pe"])
            d7["inb"] = ("pe", p.cnt["pe"])
        for j, i in enumerate(tiles):
            sl = i % 2
            g_t = xt2[:, sl * D:(sl + 1) * D]
            if dbg_moe is not None:
                p.dma("gpsimd", dbg_moe[i * 128:(i + 1) * 128, :],
                      accb[:, j * D:(j + 1) * D],
                      after=[d7[("acc", j, 0)], d7[("acc", j, 1)]])
            tsi = p.op("scalar", lambda h, j=j, g_t=g_t: h.activation(
                out=g_t, in_=accb[:, j * D:(j + 1) * D], func=AF.Silu),
                after=[d7[("acc", j, 0)], d7[("acc", j, 1)], d7.get(("g", sl))],
                inc="act")
            tls = p.dma("sync", xt1[:, sl * D:(sl + 1) * D],
                        s1_scr[i * 128:(i + 1) * 128, :],
                        after=s1_stores + [xn_stored.get(i), d7.get(("x1", sl)),
                                           router_done[-1]])
            tm1 = p.op("vector", lambda h, g_t=g_t, sl=sl: h.tensor_tensor(
                g_t, g_t, xt1[:, sl * D:(sl + 1) * D], ALU.mult),
                after=[tsi, tls], inc="dve")
            tl2 = p.dma("sync", xt1[:, sl * D:(sl + 1) * D],
                        x1_in[i * 128:(i + 1) * 128, :], after=[tm1])
            tm2 = p.op("vector", lambda h, g_t=g_t, sl=sl: h.tensor_tensor(
                g_t, g_t, xt1[:, sl * D:(sl + 1) * D], ALU.add),
                after=[tl2], inc="dve")
            tst = p.dma("sync", out_out[i * 128:(i + 1) * 128, :], g_t, after=[tm2])
            out_stores.append(tst)
            d7[("x1", sl)] = tm2
            d7[("g", sl)] = tst
            d7[("acc", j, 0)] = tsi if H == 0 else d7[("acc", j, 0)]
            d7[("acc", j, 1)] = tsi if H == 0 else d7[("acc", j, 1)]
    p.op("gpsimd", None, after=out_stores)
    p.emit_and_close()
    return nc



def kernel(**inputs):
    g = host_prep(inputs)
    NV = g["NV"]
    vc = {n: i for i, n in enumerate(g["vec_names"])}
    Nt, CS, B, T = 2048, 256, 2, 8192
    cores = list(range(8))
    bq = [divmod(c, 4) for c in cores]

    def launch(nc, maps):
        res = run_bass_kernel_spmd(nc, maps, core_ids=cores)
        return res.results

    r0 = launch(build_p0(Nt, NV, vc), [{
        "u_blk": np.ascontiguousarray(g["u_pad"][b, q * Nt: q * Nt + 128 + Nt]),
        "W_inT": g["W_inT"], "vecs": g["vecs"]}
        for (b, q) in bq])
    xv = np.zeros((B, 1024, T), np.float32)
    x2 = np.zeros((B, 1024, T), np.float32)
    xnfm = np.zeros((B, 1024, T), ml_dtypes.bfloat16)
    xntm = np.zeros((B, T, 1024), np.float32)
    for c, (b, q) in enumerate(bq):
        t0 = q * Nt
        xv[b, :, t0:t0 + Nt] = r0[c]["xv_fm"]
        x2[b, :, t0:t0 + Nt] = r0[c]["x2_fm"]
        xnfm[b, :, t0:t0 + Nt] = r0[c]["xn0_fm_b"]
        xntm[b, t0:t0 + Nt] = r0[c]["xn0_tm"]

    r1 = launch(build_p1(CS, NV, vc), [{
        "xv_fm": np.ascontiguousarray(xv[b, q * CS:(q + 1) * CS]),
        "x2_fm": np.ascontiguousarray(x2[b, q * CS:(q + 1) * CS]),
        "vecs": np.ascontiguousarray(g["vecs"][q * CS:(q + 1) * CS])}
        for (b, q) in bq])
    y0g = np.zeros((B, 1024, T), np.float32)
    for c, (b, q) in enumerate(bq):
        y0g[b, q * CS:(q + 1) * CS] = r1[c]["y0g_fm"]

    u = np.asarray(inputs["L0_attn_u"])
    r2 = launch(build_p2(Nt, NV, vc), [{
        "y0g_fm": np.ascontiguousarray(y0g[b, :, q * Nt:(q + 1) * Nt]),
        "u_rows": np.ascontiguousarray(u[b, q * Nt:(q + 1) * Nt]),
        "W_outT": g["W_outT"], "l1T": g["l1T"], "l2T": g["l2T"], "l3T": g["l3T"],
        "vecs": g["vecs"]} for (b, q) in bq])
    s0 = np.zeros((B, T, 1024), np.float32)
    for c, (b, q) in enumerate(bq):
        s0[b, q * Nt:(q + 1) * Nt] = r2[c]["s0_tm"]

    r3 = launch(build_p3(Nt), [{
        "xn0_fm_b": np.ascontiguousarray(xnfm[b, :, q * Nt:(q + 1) * Nt]),
        "q_w": g["q_w"], "k_w": g["k_w"], "v_w": g["v_w"]} for (b, q) in bq])
    kfm = np.zeros((B, 1024, T), ml_dtypes.bfloat16)
    vtm = np.zeros((B, T, 1024), ml_dtypes.bfloat16)
    qfm = np.zeros((B, 1024, T), ml_dtypes.bfloat16)
    for c, (b, q) in enumerate(bq):
        qfm[b, :, q * Nt:(q + 1) * Nt] = r3[c]["q_fm_b"]
        kfm[b, :, q * Nt:(q + 1) * Nt] = r3[c]["k_fm_b"]
        vtm[b, q * Nt:(q + 1) * Nt] = r3[c]["v_tm_b"]

    swT2 = np.ascontiguousarray(
        np.vstack([g["scale_w"].T, g["scale_w"].T]).astype(np.float32))
    p4rows = np.ascontiguousarray(np.stack(
        [inputs["L0_ffn_n1_g"], inputs["L0_ffn_n1_b"],
         inputs["L0_ffn_n2_g"], inputs["L0_ffn_n2_b"]]).astype(np.float32))
    mrows = np.ascontiguousarray(np.stack(
        [inputs["L0_ffn_moe_gate_b"], inputs["L0_ffn_res_scale"]]).astype(np.float32))
    r4 = launch(build_p4(Nt, NV, vc), [{
        "q_fm_b": np.ascontiguousarray(qfm[b, :, q * Nt:(q + 1) * Nt]),
        "k_fm_b": np.ascontiguousarray(kfm[b]),
        "v_tm_b": np.ascontiguousarray(vtm[b]),
        "scale_wT2": swT2,
        "xn0_tm": np.ascontiguousarray(xntm[b, q * Nt:(q + 1) * Nt]),
        "s0_tm": np.ascontiguousarray(s0[b, q * Nt:(q + 1) * Nt]),
        "u_rows": np.ascontiguousarray(u[b, q * Nt:(q + 1) * Nt]),
        "gate_w": g["gate_w"], "out_w": g["out_w"], "vecs": g["vecs"],
        "p4rows": p4rows, "moe_gate_w": g["moe_gate_w"], "moe_rows": mrows,
        "w1": g["w1"], "b1": g["b1"], "ln_g": g["ln_g"], "ln_b": g["ln_b"],
        "w2": g["w2"], "b2": g["b2"]} for (b, q) in bq])
    x1 = np.zeros((B, T, 1024), np.float32)
    for c, (b, q) in enumerate(bq):
        x1[b, q * Nt:(q + 1) * Nt] = r4[c]["x1_tm"]

    p5rows = np.ascontiguousarray(np.stack(
        [inputs["L1_attn_ln0_w"], inputs["L1_attn_ln0_b"],
         inputs["L1_attn_ln1_w"], inputs["L1_attn_ln1_b"]]).astype(np.float32))
    maps5 = []
    for (b, q) in bq:
        t0 = q * Nt
        ext = np.zeros((128 + Nt, 1024), np.float32)
        lo = max(0, t0 - 128)
        ext[128 - (t0 - lo):] = x1[b, lo:t0 + Nt]
        maps5.append({
            "x1_ext": ext,
            "halo_valid": np.array([[1.0 if t0 > 0 else 0.0]], np.float32),
            "WkT": g["WkT"], "WvT": g["WvT"], "WrT": g["WrT"],
            "vecs": g["vecs"], "p5rows": p5rows})
    r5 = launch(build_p5(Nt, NV, vc), maps5)
    k1 = np.zeros((B, 1024, T), np.float32)
    v1 = np.zeros((B, 1024, T), np.float32)
    sr = np.zeros((B, 1024, T), np.float32)
    xp = np.zeros((B, T, 1024), np.float32)
    xn1 = np.zeros((B, T, 1024), np.float32)
    for c, (b, q) in enumerate(bq):
        t0 = q * Nt
        k1[b, :, t0:t0 + Nt] = r5[c]["k1_fm"]
        v1[b, :, t0:t0 + Nt] = r5[c]["v1_fm"]
        sr[b, :, t0:t0 + Nt] = r5[c]["sr_fm"]
        xp[b, t0:t0 + Nt] = r5[c]["xp_tm"]
        xn1[b, t0:t0 + Nt] = r5[c]["xn1_tm"]

    r6 = launch(build_p6(CS, NV, vc), [{
        "k1_fm": np.ascontiguousarray(k1[b, q * CS:(q + 1) * CS]),
        "v1_fm": np.ascontiguousarray(v1[b, q * CS:(q + 1) * CS]),
        "vecs": np.ascontiguousarray(g["vecs"][q * CS:(q + 1) * CS])}
        for (b, q) in bq])
    ya = np.zeros((B, 1024, T), np.float32)
    for c, (b, q) in enumerate(bq):
        ya[b, q * CS:(q + 1) * CS] = r6[c]["y_att_fm"]

    p7rows = np.ascontiguousarray(np.stack(
        [inputs["L1_attn_ln2_w"], inputs["L1_attn_ln2_b"]]).astype(np.float32))
    maps7 = []
    for (b, q) in bq:
        t0 = q * Nt
        e0 = t0 - 128
        if e0 >= 0:
            yae = ya[b, :, e0:t0 + Nt]
            sre = sr[b, :, e0:t0 + Nt]
            xpe = xp[b, e0:t0 + Nt]
            hv = 1.0
        else:
            yae = np.pad(ya[b, :, 0:t0 + Nt], ((0, 0), (128, 0)))
            sre = np.pad(sr[b, :, 0:t0 + Nt], ((0, 0), (128, 0)))
            xpe = np.pad(xp[b, 0:t0 + Nt], ((128, 0), (0, 0)))
            hv = 0.0
        maps7.append({
            "ya_ext": np.ascontiguousarray(yae),
            "sr_ext": np.ascontiguousarray(sre),
            "xp_ext": np.ascontiguousarray(xpe),
            "xn1_rows": np.ascontiguousarray(xn1[b, t0:t0 + Nt]),
            "x1_rows": np.ascontiguousarray(x1[b, t0:t0 + Nt]),
            "halo_valid": np.array([[hv]], np.float32),
            "WoT": g["WoT"], "WfkT": g["WfkT"], "WfrT": g["WfrT"],
            "WfvT": g["WfvT"], "vecs": g["vecs"], "p7rows": p7rows,
            "f_gateT": g["f_gateT"], "f_w1": g["f_w1"], "f_w2": g["f_w2"],
            "f_w3": g["f_w3"], "f_w1s": g["f_w1s"], "f_w2s": g["f_w2s"],
            "f_w3s": g["f_w3s"]})
    r7 = launch(build_p7(Nt, NV, vc), maps7)
    out = np.zeros((B, T, 1024), np.float32)
    for c, (b, q) in enumerate(bq):
        out[b, q * Nt:(q + 1) * Nt] = r7[c]["out_rows"]
    return out
